# version 26
# baseline (speedup 1.0000x reference)
import numpy as np
import ml_dtypes
import concourse.bass as bass
import concourse.mybir as mybir
from concourse.bass_utils import run_bass_kernel_spmd

F32 = mybir.dt.float32
BF16 = mybir.dt.bfloat16
BF = ml_dtypes.bfloat16
AL = mybir.AluOpType
ACTF = mybir.ActivationFunctionType
AXX = mybir.AxisListType.X

D = 1024
NSEQ = 4096
TOWN = 2048
GROUPS = 8
OWN_GROUPS = 4
PAIRS = 8
KCH = 32
QB = 4
CR = float(np.float32(1.5 * 2 ** 23))
EPS = 1e-5
MHAT = 25.0


class Plan:
    ENGS = ("sync", "gpsimd", "tensor", "vector", "scalar")

    def __init__(self):
        self.q = {e: [] for e in self.ENGS}
        self.n = {}

    def inc(self, sem, k=1):
        self.n[sem] = self.n.get(sem, 0) + k
        return self.n[sem]

    def val(self, sem):
        return self.n.get(sem, 0)

    def add(self, eng, fn):
        self.q[eng].append(fn)

    def wait(self, eng, sem, v):
        if v > 0:
            self.add(eng, lambda e, s=sem, vv=v: e.wait_ge(s, vv))

    def op(self, eng, build, inc=None, inck=1):
        if inc is not None:
            self.inc(inc, inck)
            self.add(eng, lambda e, b=build, s=inc, k=inck: b(e).then_inc(s, k))
        else:
            self.add(eng, lambda e, b=build: b(e))


def build_nc(stage="full"):
    nc = bass.Bass("TRN2", target_bir_lowering=False, debug=False)

    x_d = nc.dram_tensor("x", [NSEQ, D], F32, kind="ExternalInput").ap()
    wqt_d = nc.dram_tensor("wqt", [D, D], F32, kind="ExternalInput").ap()
    wkt_d = nc.dram_tensor("wkt", [D, D], F32, kind="ExternalInput").ap()
    wvt_d = nc.dram_tensor("wvt", [D, D], F32, kind="ExternalInput").ap()
    wot_d = nc.dram_tensor("wot", [D, D], F32, kind="ExternalInput").ap()
    aqkv_d = nc.dram_tensor("aqkvt", [D, 96], F32, kind="ExternalInput").ap()
    aot_d = nc.dram_tensor("aot", [D, 16], F32, kind="ExternalInput").ap()
    bqt_d = nc.dram_tensor("bqt", [16, D], F32, kind="ExternalInput").ap()
    bkt_d = nc.dram_tensor("bkt", [16, D], F32, kind="ExternalInput").ap()
    bvt_d = nc.dram_tensor("bvt", [16, D], F32, kind="ExternalInput").ap()
    bot_d = nc.dram_tensor("bot", [16, D], F32, kind="ExternalInput").ap()
    rhst_d = nc.dram_tensor("rhst", [D, D], F32, kind="ExternalInput").ap()
    bof_d = nc.dram_tensor("bo_full", [128, D], F32, kind="ExternalInput").ap()
    out_d = nc.dram_tensor("out", [TOWN, D], F32, kind="ExternalOutput").ap()

    kt_h = nc.dram_tensor("kt_h", [D, NSEQ], BF16, kind="Internal").ap()
    v_h = nc.dram_tensor("v_h", [NSEQ, D], BF16, kind="Internal").ap()

    ident_d = nc.inline_tensor(np.eye(128, dtype=np.float32), name="ident")

    x_r = x_d.rearrange("(a p) d -> p a d", p=128)
    v_h_r = v_h.rearrange("(a p) o -> p a o", p=128)
    out_r = out_d.rearrange("(a p) d -> p a d", p=128)

    UW = 27648

    from contextlib import ExitStack
    ctx = ExitStack()
    sb = lambda *a: ctx.enter_context(nc.sbuf_tensor(*a))
    pst = lambda *a: ctx.enter_context(nc.psum_tensor(*a))
    sem = lambda n: ctx.enter_context(nc.semaphore(n))
    with ctx:
        qt = sb("qt", [128, 8 * 2048], BF16)
        qlt = sb("qlt", [128, 8 * 2048], BF16)
        qrh = sb("qrh", [128, 8 * 1024], BF16)
        rs_full = sb("rs_full", [128, 1024], F32)
        ls_all = sb("ls_all", [128, 16], F32)
        stats = sb("stats", [128, 160], F32)
        ones_b = sb("ones_b", [128, 64], BF16)
        ones_f = sb("ones_f", [128, 128], F32)
        ident_f = sb("ident_f", [128, 128], F32)
        biasx = sb("biasx", [128, 4], F32)
        aqkv_s = sb("aqkv_s", [128, 768], BF16)
        aot_s = sb("aot_s", [128, 128], BF16)
        bqt_s = sb("bqt_s", [128, 1024], BF16)
        bkt_s = sb("bkt_s", [128, 1024], BF16)
        bvt_s = sb("bvt_s", [128, 1024], BF16)
        bot_s = sb("bot_s", [128, 1024], BF16)
        t1s = sb("t1s", [128, 512], BF16)
        U = sb("U", [128, UW], F32)
        ps_s0 = pst("ps_s0", [128, 1024], F32)
        ps_s1 = pst("ps_s1", [128, 1024], F32)
        ps_s2 = pst("ps_s2", [128, 1024], F32)
        ps_pv = pst("ps_pv", [128, 512], F32)
        ps_l = pst("ps_l", [128, 512], F32)
        sLd = sem("sLd")
        sCa = sem("sCa")
        sSt = sem("sSt")
        sPE = sem("sPE")
        sVE = sem("sVE")
        sAC = sem("sAC")
        block = ctx.enter_context(nc.Block())

        P = Plan()

        def uf(off, n):
            ap = U[:, off:off + n]
            return ap

        def ui(off, n):
            ap = U[:, off:off + n].bitcast(mybir.dt.int32)
            return ap

        def ub(off, nbf):
            ap = U[:, off:off + nbf // 2].bitcast(BF16)
            assert tuple(ap.shape) == (128, nbf), ap.shape
            return ap

        P.op("sync", lambda e: e.dma_start(out=ident_f[:, :], in_=ident_d.ap()), inc=sLd, inck=16)
        rhst_v = uf(12288, 8192)
        P.op("sync", lambda e: e.dma_start(
            out=rhst_v.rearrange("p (c i) -> p c i", c=8),
            in_=rhst_d.rearrange("(c p) i -> p c i", p=128)), inc=sLd, inck=16)
        init_ld = P.val(sLd)
        P.op("vector", lambda e: e.memset(biasx[:, :], -MHAT))
        P.op("vector", lambda e: e.memset(ones_b[:, :], 1.0))
        P.op("vector", lambda e: e.memset(ones_f[:, :], 1.0), inc=sVE)
        init_ve = P.val(sVE)

        a1_ev = {}
        P.wait("vector", sLd, init_ld)
        P.op("vector", lambda e: e.reduce_max(
            stats[:, 0:8], rhst_v.rearrange("p (c i) -> p c i", c=8),
            axis=AXX, apply_absolute_value=True), inc=sVE)
        P.wait("vector", sVE, P.val(sVE))
        P.op("vector", lambda e: e.tensor_scalar_max(stats[:, 8:16], stats[:, 0:8], 1e-12), inc=sVE)
        P.wait("vector", sVE, P.val(sVE))
        P.op("vector", lambda e: e.reciprocal(stats[:, 16:24], stats[:, 8:16]), inc=sVE)
        P.wait("vector", sVE, P.val(sVE))
        P.op("vector", lambda e: e.tensor_scalar_mul(stats[:, 24:32], stats[:, 16:24], 127.0))
        P.op("vector", lambda e: e.tensor_scalar_mul(stats[:, 32:40], stats[:, 8:16], 1.0 / 127.0), inc=sVE)
        P.wait("vector", sVE, P.val(sVE))
        qsr2 = [uf(20480, 1024), uf(22656, 1024)]
        rst8 = uf(21504, 128)
        rs_row = uf(21632, 1024)
        a1_tp = {}
        a1_rhst_ve = 0
        for j in range(8):
            qsr = qsr2[j % 2]
            if j >= 2:
                P.wait("vector", sPE, a1_tp[j - 2])
            qsri = qsr.bitcast(mybir.dt.int32)
            P.op("vector", lambda e, j=j, qsri=qsri: e.tensor_scalar_mul(
                qsri, rhst_v[:, j * 1024:(j + 1) * 1024],
                stats[:, 24 + j:25 + j]), inc=sVE)
            P.wait("vector", sVE, P.val(sVE))
            P.op("vector", lambda e, qsr=qsr, qsri=qsri: e.tensor_copy(qsr[:, :], qsri), inc=sVE)
            if j == 7:
                a1_rhst_ve = P.val(sVE)
            P.wait("tensor", sVE, P.val(sVE))
            if j >= 2:
                P.wait("tensor", sVE, a1_ev[j - 2])
            for ic in range(8):
                P.op("tensor", lambda e, ic=ic, qsr=qsr: e.transpose(
                    ps_s2[:, ic * 128:(ic + 1) * 128], qsr[:, ic * 128:(ic + 1) * 128],
                    ident_f[:, :]), inc=(sPE if ic == 7 else None))
            a1_tp[j] = P.val(sPE)
            P.wait("vector", sPE, P.val(sPE))
            P.op("vector", lambda e, j=j: e.tensor_copy(
                qrh[:, :].rearrange("p (c o) -> p c o", c=8)[:, :, j * 128:(j + 1) * 128],
                ps_s2[:, :].rearrange("p (c o) -> p c o", c=8)), inc=sVE)
            a1_ev[j] = P.val(sVE)
        P.wait("tensor", sVE, P.val(sVE))
        P.op("tensor", lambda e: e.transpose(ps_s2[0:8, 0:128], stats[:, 32:40], ident_f[:, :]), inc=sPE)
        P.wait("vector", sPE, P.val(sPE))
        P.op("vector", lambda e: e.tensor_copy(rst8[0:8, 0:128], ps_s2[0:8, 0:128]), inc=sVE)
        P.wait("sync", sVE, P.val(sVE))
        for j in range(8):
            P.op("sync", lambda e, j=j: e.dma_start(
                out=rs_row[0:1, j * 128:(j + 1) * 128], in_=rst8[j:j + 1, 0:128]),
                inc=sSt, inck=16)
        P.wait("tensor", sSt, P.val(sSt))
        P.op("tensor", lambda e: e.matmul(ps_s2[:, 0:512], ones_f[0:1, 0:128],
                                          rs_row[0:1, 0:512], start=True, stop=True))
        P.op("tensor", lambda e: e.matmul(ps_s2[:, 512:1024], ones_f[0:1, 0:128],
                                          rs_row[0:1, 512:1024], start=True, stop=True), inc=sPE)
        P.wait("vector", sPE, P.val(sPE))
        P.op("vector", lambda e: e.tensor_copy(rs_full[:, :], ps_s2[:, :]), inc=sVE)
        a1_ve = P.val(sVE)

        wq_v = ub(0, 8192)
        wk_v = ub(4096, 8192)
        wv_v = ub(8192, 8192)
        xg = [uf(12288, 4096), uf(16384, 4096)]
        xnT2 = [ub(20480, 4096), ub(22528, 4096)]
        qs2b = [uf(24576, 1024), uf(25600, 1024)]
        kvst = [ub(26624, 512), ub(26880, 512), ub(27136, 512), ub(27392, 512)]

        wload = {}
        for nm, dst, src_ in (("k", wk_v, wkt_d), ("v", wv_v, wvt_d)):
            P.op("gpsimd", lambda e, d=dst, s=src_: e.dma_start(
                out=d.rearrange("p (c o) -> p c o", c=8),
                in_=s.rearrange("(c p) o -> p c o", p=128)), inc=sCa, inck=16)
            wload[nm] = P.val(sCa)
        P.op("gpsimd", lambda e: e.dma_start(
            out=aqkv_s[:, :].rearrange("p (c r) -> p c r", c=8),
            in_=aqkv_d.rearrange("(c p) r -> p c r", p=128)), inc=sCa, inck=16)
        P.op("gpsimd", lambda e: e.dma_start(
            out=aot_s[:, :].rearrange("p (c r) -> p c r", c=8),
            in_=aot_d.rearrange("(c p) r -> p c r", p=128)), inc=sCa, inck=16)
        for dst, src_, r0 in ((bqt_s, bqt_d, 0), (bkt_s, bkt_d, 32),
                              (bvt_s, bvt_d, 64), (bot_s, bot_d, 0)):
            P.op("gpsimd", lambda e, d=dst, s=src_, r0=r0: e.dma_start(
                out=d[r0:r0 + 16, :], in_=s), inc=sCa, inck=16)
        lora_ca = P.val(sCa)
        P.op("gpsimd", lambda e: e.dma_start(
            out=wq_v.rearrange("p (c o) -> p c o", c=8),
            in_=wqt_d.rearrange("(c p) o -> p c o", p=128)), inc=sCa, inck=16)
        wload["q"] = P.val(sCa)

        xload_ld = {}
        xg_free = {}

        def emit_xload(g):
            if g < 2:
                P.wait("sync", sVE, a1_rhst_ve)
            if g >= 2:
                pe_v, ve_v = xg_free[g - 2]
                P.wait("sync", sPE, pe_v)
                P.wait("sync", sVE, ve_v)
            P.op("sync", lambda e, g=g: e.dma_start(
                out=xg[g % 2].rearrange("p (a d) -> p a d", a=4),
                in_=x_r[:, 4 * g:4 * g + 4, :]), inc=sLd, inck=16)
            xload_ld[g] = P.val(sLd)

        emit_xload(0)
        emit_xload(1)

        ps_slots = [ps_s0[:, 0:512], ps_s0[:, 512:1024], ps_s1[:, 0:512], ps_s1[:, 512:1024]]
        slot_evac = [0, 0, 0, 0]
        slot_i = [0]
        stage_hist = []
        s2_free_ac = [0]

        def proj_tile(mms, evac_fn):
            si = slot_i[0] % 4
            slot_i[0] += 1
            ps = ps_slots[si]
            P.wait("tensor", sAC, slot_evac[si])
            n = len(mms)
            for i, mm in enumerate(mms):
                lh, rh = mm[0], mm[1]
                tp = mm[2] if len(mm) > 2 else None
                P.op("tensor", lambda e, lh=lh, rh=rh, i=i, ps=ps, n=n, tp=tp:
                     e.matmul(ps, lh, rh, start=(i == 0), stop=(i == n - 1),
                              tile_position=tp),
                     inc=(sPE if i == n - 1 else None))
            P.wait("scalar", sPE, P.val(sPE))
            P.op("scalar", lambda e, ps=ps: evac_fn(e, ps), inc=sAC)
            slot_evac[si] = P.val(sAC)

        def lora_t1(xnT, ready_ac):
            P.wait("tensor", sCa, lora_ca)
            P.wait("tensor", sAC, ready_ac)
            for ic in range(8):
                P.op("tensor", lambda e, ic=ic, xnT=xnT: e.matmul(
                    ps_pv[0:96, :], aqkv_s[:, ic * 96:(ic + 1) * 96],
                    xnT[:, ic * 512:(ic + 1) * 512],
                    start=(ic == 0), stop=(ic == 7)), inc=(sPE if ic == 7 else None))
            P.wait("scalar", sPE, P.val(sPE))
            P.op("scalar", lambda e: e.mul(t1s[0:96, :], ps_pv[0:96, :], 1.0 / 16.0), inc=sAC)
            return P.val(sAC)

        def kv_stage_store():
            idx = len(stage_hist)
            st = kvst[idx % 4]
            wait_st = stage_hist[idx - 4] if idx >= 4 else 0
            P.wait("scalar", sSt, wait_st)

            def evac(e, ps, st=st):
                return e.copy(st[:, :], ps)
            return st, evac

        s2_half = [0, 0]
        s2_par = [0]

        def transpose_batch(src_ap, dst_fn, in_ve):
            P.wait("tensor", sVE, in_ve)
            tp = 0
            for h in range(2):
                par = s2_par[0]
                s2_par[0] ^= 1
                P.wait("tensor", sAC, s2_half[par])
                for ic in range(4):
                    icg = h * 4 + ic
                    P.op("tensor", lambda e, icg=icg, ic=ic, s=src_ap, par=par: e.transpose(
                        ps_s2[:, par * 512 + ic * 128: par * 512 + (ic + 1) * 128],
                        s[:, icg * 128:(icg + 1) * 128],
                        ident_f[:, :]), inc=(sPE if ic == 3 else None))
                tp = P.val(sPE)
                P.wait("scalar", sPE, tp)
                P.op("scalar", lambda e, h=h, par=par: dst_fn(
                    e, h, ps_s2[:, par * 512:(par + 1) * 512]), inc=sAC)
                s2_half[par] = P.val(sAC)
            return tp

        qtrans_pe = {}

        def fe_stats(g):
            b0 = 40 + 56 * (g % 2)
            P.wait("vector", sLd, xload_ld[g])
            xgv = xg[g % 2]
            for tt in range(4):
                P.op("vector", lambda e, tt=tt, xgv=xgv, b0=b0: e.scalar_tensor_tensor(
                    qs2b[0][:, :], xgv[:, tt * 1024:(tt + 1) * 1024], 1.0,
                    xgv[:, tt * 1024:(tt + 1) * 1024], AL.bypass, AL.mult,
                    accum_out=stats[:, b0 + tt:b0 + tt + 1]))
            P.op("vector", lambda e, xgv=xgv, b0=b0: e.reduce_sum(
                stats[:, b0 + 4:b0 + 8], xgv.rearrange("p (a d) -> p a d", a=4), axis=AXX), inc=sVE)
            P.wait("vector", sVE, P.val(sVE))
            P.op("vector", lambda e, b0=b0: e.tensor_scalar_mul(
                stats[:, b0 + 8:b0 + 12], stats[:, b0 + 4:b0 + 8], 1.0 / D))
            P.op("vector", lambda e, b0=b0: e.tensor_scalar(
                stats[:, b0 + 12:b0 + 16], stats[:, b0:b0 + 4], 1.0 / D, EPS,
                AL.mult, AL.add), inc=sVE)
            P.wait("vector", sVE, P.val(sVE))
            P.op("vector", lambda e, b0=b0: e.tensor_mul(
                stats[:, b0 + 16:b0 + 20], stats[:, b0 + 8:b0 + 12], stats[:, b0 + 8:b0 + 12]), inc=sVE)
            P.wait("vector", sVE, P.val(sVE))
            P.op("vector", lambda e, b0=b0: e.tensor_sub(
                stats[:, b0 + 20:b0 + 24], stats[:, b0 + 12:b0 + 16], stats[:, b0 + 16:b0 + 20]), inc=sVE)
            P.wait("vector", sVE, P.val(sVE))
            P.op("vector", lambda e, b0=b0: e.reciprocal(
                stats[:, b0 + 24:b0 + 28], stats[:, b0 + 20:b0 + 24]), inc=sVE)
            P.wait("scalar", sVE, P.val(sVE))
            P.op("scalar", lambda e, b0=b0: e.activation(
                stats[:, b0 + 28:b0 + 32], stats[:, b0 + 24:b0 + 28], ACTF.Sqrt), inc=sAC)
            P.wait("vector", sAC, P.val(sAC))
            P.op("vector", lambda e, b0=b0: e.tensor_mul(
                stats[:, b0 + 32:b0 + 36], stats[:, b0 + 28:b0 + 32], stats[:, b0 + 28:b0 + 32]), inc=sVE)
            P.wait("vector", sVE, P.val(sVE))
            P.op("vector", lambda e, b0=b0: e.tensor_mul(
                stats[:, b0 + 36:b0 + 40], stats[:, b0 + 32:b0 + 36], stats[:, b0 + 20:b0 + 24]), inc=sVE)
            P.wait("vector", sVE, P.val(sVE))
            P.op("vector", lambda e, b0=b0: e.tensor_scalar(
                stats[:, b0 + 40:b0 + 44], stats[:, b0 + 36:b0 + 40], -0.5, 1.5,
                AL.mult, AL.add), inc=sVE)
            P.wait("vector", sVE, P.val(sVE))
            P.op("vector", lambda e, b0=b0: e.tensor_mul(
                stats[:, b0 + 44:b0 + 48], stats[:, b0 + 28:b0 + 32], stats[:, b0 + 40:b0 + 44]), inc=sVE)
            P.wait("vector", sVE, P.val(sVE))

        def frontend(g):
            own = g < OWN_GROUPS
            b0 = 40 + 56 * (g % 2)
            xnT = xnT2[g % 2]
            xgv = xg[g % 2]
            xn_ve = {}
            for tt in range(4):
                P.op("vector", lambda e, tt=tt, xgv=xgv, b0=b0: e.tensor_scalar(
                    xgv[:, tt * 1024:(tt + 1) * 1024], xgv[:, tt * 1024:(tt + 1) * 1024],
                    stats[:, b0 + 8 + tt:b0 + 9 + tt], stats[:, b0 + 44 + tt:b0 + 45 + tt],
                    AL.subtract, AL.mult), inc=sVE)
                xn_ve[tt] = P.val(sVE)

            last_tp = 0
            for tt in range(4):
                def evac_xn(e, h, psh, tt=tt, xnT=xnT):
                    return e.copy(
                        xnT.rearrange("p (c t) -> p c t", c=8)[:, 4 * h:4 * h + 4,
                                                               tt * 128:(tt + 1) * 128],
                        psh.rearrange("p (c t) -> p c t", c=4))
                last_tp = transpose_batch(xgv[:, tt * 1024:(tt + 1) * 1024], evac_xn, xn_ve[tt])
            xnT_ac_of[g] = P.val(sAC)

            quant_ve = 0
            if own:
                for tt in range(4):
                    gt = g * 4 + tt
                    qsb = qs2b[tt % 2]
                    if tt >= 2:
                        P.wait("vector", sPE, qtrans_pe[(g, tt - 2)])
                    P.op("vector", lambda e, tt=tt, xgv=xgv: e.reduce_max(
                        stats[:, 152:153], xgv[:, tt * 1024:(tt + 1) * 1024], axis=AXX,
                        apply_absolute_value=True), inc=sVE)
                    P.wait("vector", sVE, P.val(sVE))
                    P.op("vector", lambda e: e.tensor_scalar_max(
                        stats[:, 153:154], stats[:, 152:153], 1e-12), inc=sVE)
                    P.wait("vector", sVE, P.val(sVE))
                    P.op("vector", lambda e: e.reciprocal(stats[:, 154:155], stats[:, 153:154]), inc=sVE)
                    P.wait("vector", sVE, P.val(sVE))
                    P.op("vector", lambda e: e.tensor_scalar_mul(stats[:, 155:156], stats[:, 154:155], 127.0))
                    P.op("vector", lambda e, gt=gt: e.tensor_scalar_mul(
                        ls_all[:, gt:gt + 1], stats[:, 153:154], 1.0 / 127.0), inc=sVE)
                    P.wait("vector", sVE, P.val(sVE))
                    qsbi = qsb.bitcast(mybir.dt.int32)
                    P.op("vector", lambda e, tt=tt, xgv=xgv, qsbi=qsbi: e.tensor_scalar_mul(
                        qsbi, xgv[:, tt * 1024:(tt + 1) * 1024],
                        stats[:, 155:156]), inc=sVE)
                    P.wait("vector", sVE, P.val(sVE))
                    P.op("vector", lambda e, qsb=qsb, qsbi=qsbi: e.tensor_copy(qsb[:, :], qsbi), inc=sVE)
                    qv = P.val(sVE)

                    def evac_q(e, h, psh, gt=gt):
                        return e.copy(
                            qlt.rearrange("p (c t) -> p c t", c=8)[:, 4 * h:4 * h + 4,
                                                                   gt * 128:(gt + 1) * 128],
                            psh.rearrange("p (c t) -> p c t", c=4))
                    qtrans_pe[(g, tt)] = transpose_batch(qsb, evac_q, qv)
                quant_ve = P.val(sVE)
            xg_free[g] = (qtrans_pe[(g, 3)] if own else last_tp, quant_ve)
            if g + 2 < GROUPS:
                emit_xload(g + 2)


        def backend(g):
            own = g < OWN_GROUPS
            xnT = xnT2[g % 2]
            xnT_ac = xnT_ac_of[g]

            t1_ac = lora_t1(xnT, xnT_ac)
            P.wait("tensor", sAC, t1_ac)
            P.wait("tensor", sCa, wload["k"])
            for oc in range(8):
                mms = [(wk_v[:, ic * 1024 + oc * 128: ic * 1024 + (oc + 1) * 128],
                        xnT[:, ic * 512:(ic + 1) * 512]) for ic in range(8)]
                mms.append((bkt_s[32:48, oc * 128:(oc + 1) * 128], t1s[32:48, :], (32, 0)))
                st, evac = kv_stage_store()
                proj_tile(mms, evac)
                P.wait("sync", sAC, P.val(sAC))
                P.op("sync", lambda e, oc=oc, g=g, st=st: e.dma_start(
                    out=kt_h[oc * 128:(oc + 1) * 128, g * 512:(g + 1) * 512],
                    in_=st[:, :]), inc=sSt, inck=16)
                stage_hist.append(P.val(sSt))

            P.wait("tensor", sCa, wload["v"])
            for tt in range(4):
                for oh in range(2):
                    mms = [(xnT[:, ic * 512 + tt * 128: ic * 512 + (tt + 1) * 128],
                            wv_v[:, ic * 1024 + oh * 512: ic * 1024 + (oh + 1) * 512])
                           for ic in range(8)]
                    mms.append((t1s[64:80, tt * 128:(tt + 1) * 128],
                                bvt_s[64:80, oh * 512:(oh + 1) * 512], (64, 0)))
                    st, evac = kv_stage_store()
                    proj_tile(mms, evac)
                    P.wait("sync", sAC, P.val(sAC))
                    P.op("sync", lambda e, g=g, tt=tt, oh=oh, st=st: e.dma_start(
                        out=v_h_r[:, g * 4 + tt, oh * 512:(oh + 1) * 512],
                        in_=st[:, :]), inc=sSt, inck=16)
                    stage_hist.append(P.val(sSt))

            if own:
                P.wait("tensor", sCa, wload["q"])
                for oc in range(8):
                    mms = [(wq_v[:, ic * 1024 + oc * 128: ic * 1024 + (oc + 1) * 128],
                            xnT[:, ic * 512:(ic + 1) * 512]) for ic in range(8)]
                    mms.append((bqt_s[0:16, oc * 128:(oc + 1) * 128], t1s[0:16, :]))

                    def evac_q2(e, ps, oc=oc, g=g):
                        return e.copy(
                            qt[:, oc * 2048 + g * 512: oc * 2048 + (g + 1) * 512], ps)
                    proj_tile(mms, evac_q2)


        xnT_ac_of = {}
        P.wait("tensor", sVE, a1_ve)
        fe_stats(0)
        frontend(0)
        fe_stats(1)
        for g in range(GROUPS):
            if g + 2 < GROUPS:
                fe_stats(g + 2)
            if g + 1 < GROUPS:
                frontend(g + 1)
            backend(g)

        a2_st = P.val(sSt)
        a2_ve = P.val(sVE)
        a2_ac = P.val(sAC)

        def dummy_out():
            P.wait("sync", sVE, P.val(sVE))
            P.wait("sync", sPE, P.val(sPE))
            P.wait("sync", sAC, P.val(sAC))
            for tt in range(16):
                P.op("sync", lambda e, tt=tt: e.dma_start(
                    out=out_r[:, tt, :], in_=rs_row[0:1, :].broadcast_to([128, 1024])
                    if False else bo_sb[:, :]), inc=sSt, inck=16)
            P.wait("sync", sSt, P.val(sSt))

        def finish_emit():
            @block.sync
            def _(e):
                for fn in P.q["sync"]:
                    fn(e)

            @block.gpsimd
            def _(e):
                for fn in P.q["gpsimd"]:
                    fn(e)

            @block.tensor
            def _(e):
                for fn in P.q["tensor"]:
                    fn(e)

            @block.vector
            def _(e):
                for fn in P.q["vector"]:
                    fn(e)

            @block.scalar
            def _(e):
                for fn in P.q["scalar"]:
                    fn(e)

        if stage == "A":
            dummy_out()
            finish_emit()
            return nc

        ktb = [ub(0, 4096), ub(2048, 4096)]
        vsb = [ub(4096, 4096), ub(6144, 4096)]
        pbuf = [ub(8192, 1024), ub(8704, 1024), ub(9216, 1024)]
        l_sp = uf(10240, 512)
        rl_full = uf(10752, 512)
        numt = ub(11264, 16384)

        kv_load_ld = {}

        def emit_kv_load(p, after_pe):
            P.wait("sync", sSt, a2_st)
            P.wait("sync", sPE, after_pe)
            P.op("sync", lambda e, p=p: e.dma_start(
                out=ktb[p % 2][:, :], in_=kt_h[p * 128:(p + 1) * 128, :]), inc=sLd, inck=16)
            P.op("sync", lambda e, p=p: e.dma_start(
                out=vsb[p % 2][:, :].rearrange("p (a o) -> p a o", a=32),
                in_=v_h_r[:, :, p * 128:(p + 1) * 128]), inc=sLd, inck=16)
            kv_load_ld[p] = P.val(sLd)

        emit_kv_load(0, 0)
        emit_kv_load(1, 0)

        P.wait("tensor", sVE, a2_ve)
        P.wait("tensor", sAC, a2_ac)
        P.wait("scalar", sVE, init_ve)

        pair_done_pe = {}
        sring = [ps_s0, ps_s1]
        PV_banks = [ps_pv, ps_s2[:, 0:512]]
        L_banks = [ps_l, ps_s2[:, 512:1024]]
        exp_of = {}
        pcons = {}
        scores_pe = {}
        dance_done = [0, 0]
        prev_repl_pe = [0]
        pending = [None]

        def flush_dance():
            Lb, PVb, pi, p_, qb_, l_ve = pending[0]
            pending[0] = None
            P.wait("tensor", sVE, l_ve)
            P.op("tensor", lambda e, Lb=Lb: e.matmul(
                Lb[0:64, :], ones_f[0:1, 0:64], l_sp[0:1, :],
                start=True, stop=True, tile_position=(0, 0)))
            P.op("tensor", lambda e, Lb=Lb: e.matmul(
                Lb[64:128, :], ones_f[32:33, 0:64], l_sp[32:33, :],
                start=True, stop=True, tile_position=(32, 64)), inc=sPE)
            prev_repl_pe[0] = P.val(sPE)
            P.wait("vector", sPE, prev_repl_pe[0])
            P.op("vector", lambda e, Lb=Lb: e.reciprocal(rl_full[:, :], Lb[:, :]), inc=sVE)
            P.wait("vector", sVE, P.val(sVE))
            P.op("vector", lambda e, p_=p_, qb_=qb_, PVb=PVb: e.tensor_mul(
                numt[:, p_ * 2048 + qb_ * 512: p_ * 2048 + (qb_ + 1) * 512],
                PVb[:, :], rl_full[:, :]), inc=sVE)
            dance_done[pi] = P.val(sVE)

        NCH = QB * KCH
        TOT = PAIRS * NCH

        def emit_scores(c):
            p_, j_ = divmod(c, NCH)
            qb_, kc_ = divmod(j_, KCH)
            if j_ == 0:
                P.wait("tensor", sLd, kv_load_ld[p_])
            kb = ktb[p_ % 2]
            S = sring[c % 2]
            if c >= 2 and (j_ < 2):
                P.wait("tensor", sAC, exp_of[c - 2])
            qa = qt[0:64, p_ * 2048 + qb_ * 512: p_ * 2048 + (qb_ + 1) * 512]
            qbb = qt[64:128, p_ * 2048 + qb_ * 512: p_ * 2048 + (qb_ + 1) * 512]
            P.op("tensor", lambda e, kc=kc_, qa=qa, S=S, kb=kb: e.matmul(
                S[:, 0:512], kb[0:64, kc * 128:(kc + 1) * 128], qa,
                start=True, stop=True, tile_position=(0, 0)))
            P.op("tensor", lambda e, kc=kc_, qbb=qbb, S=S, kb=kb: e.matmul(
                S[:, 512:1024], kb[64:128, kc * 128:(kc + 1) * 128], qbb,
                start=True, stop=True, tile_position=(64, 0)), inc=sPE)
            scores_pe[c] = P.val(sPE)

        emit_scores(0)
        emit_scores(1)
        for c in range(TOT):
            p, j = divmod(c, NCH)
            qb, kc = divmod(j, KCH)
            pi = qb % 2
            S = sring[c % 2]
            PVb = PV_banks[pi]
            Lb = L_banks[pi]
            vb = vsb[p % 2]
            P.wait("scalar", sPE, scores_pe[c])
            pb = pbuf[c % 3]
            P.op("scalar", lambda e, S=S, pb=pb: e.activation(
                pb[:, :], S[:, :], ACTF.Exp, bias=biasx[:, 0:1], scale=0.125), inc=sAC)
            exp_of[c] = P.val(sAC)
            P.wait("tensor", sAC, exp_of[c])
            if c + 2 < TOT:
                emit_scores(c + 2)
            st = kc == 0
            sp = kc == KCH - 1
            if st:
                P.wait("tensor", sVE, dance_done[pi])
            P.op("tensor", lambda e, vb=vb, kc=kc, pb=pb, st=st, sp=sp, PVb=PVb: e.matmul(
                PVb[0:64, :], vb[:, kc * 128:kc * 128 + 64], pb[:, 0:512],
                start=st, stop=sp, tile_position=(0, 0)))
            P.op("tensor", lambda e, vb=vb, kc=kc, pb=pb, st=st, sp=sp, PVb=PVb: e.matmul(
                PVb[64:128, :], vb[:, kc * 128 + 64:kc * 128 + 128], pb[:, 512:1024],
                start=st, stop=sp, tile_position=(0, 64)))
            P.op("tensor", lambda e, pb=pb, st=st, sp=sp, Lb=Lb: e.matmul(
                Lb[0:1, :], ones_b[:, 0:1], pb[:, 0:512],
                start=st, stop=sp, tile_position=(0, 0)))
            P.op("tensor", lambda e, pb=pb, st=st, sp=sp, Lb=Lb: e.matmul(
                Lb[32:33, :], ones_b[:, 0:1], pb[:, 512:1024],
                start=st, stop=sp, tile_position=(0, 32)), inc=sPE)
            pcons[c] = P.val(sPE)
            if kc == 4 and pending[0] is not None:
                flush_dance()
            if kc == KCH - 1:
                P.wait("vector", sPE, pcons[c])
                P.wait("vector", sPE, prev_repl_pe[0])
                P.op("vector", lambda e, Lb=Lb: e.tensor_copy(l_sp[0:1, :], Lb[0:1, :]))
                P.op("vector", lambda e, Lb=Lb: e.tensor_copy(l_sp[32:33, :], Lb[32:33, :]), inc=sVE)
                pending[0] = (Lb, PVb, pi, p, qb, P.val(sVE))
            if j == NCH - 1:
                pair_done_pe[p] = P.val(sPE)
                if p == PAIRS - 1 and pending[0] is not None:
                    flush_dance()
                if p + 2 < PAIRS:
                    emit_kv_load(p + 2, pair_done_pe[p])
        b_ve = P.val(sVE)

        if stage == "B":
            dummy_out()
            finish_emit()
            return nc

        wot_v = ub(0, 8192)
        t1o = ub(4096, 2048)
        bo_sb = uf(19456, 1024)
        xf2 = [uf(5120, 1024), uf(6144, 1024)]
        xbo = uf(7168, 1024)
        oscr = uf(8192, 512)
        fscr = uf(8704, 512)
        outst = [uf(9216, 512), uf(9728, 512)]

        P.wait("gpsimd", sPE, pair_done_pe[7])
        P.op("gpsimd", lambda e: e.dma_start(
            out=wot_v.rearrange("p (c o) -> p c o", c=8),
            in_=wot_d.rearrange("(c p) o -> p c o", p=128)), inc=sCa, inck=16)
        wot_ca = P.val(sCa)
        P.wait("sync", sVE, b_ve)
        P.op("sync", lambda e: e.dma_start(out=bo_sb[:, :], in_=bof_d), inc=sLd, inck=16)
        bo_ld = P.val(sLd)

        if stage == "Cw":
            dummy_out()
            finish_emit()
            return nc

        P.wait("tensor", sVE, b_ve)
        for blk in range(4):
            P.wait("tensor", sVE, P.val(sVE))
            for oc in range(8):
                P.op("tensor", lambda e, blk=blk, oc=oc: e.matmul(
                    ps_pv[0:16, :], aot_s[:, oc * 16:(oc + 1) * 16],
                    numt[:, oc * 2048 + blk * 512: oc * 2048 + (blk + 1) * 512],
                    start=(oc == 0), stop=(oc == 7)), inc=(sPE if oc == 7 else None))
            P.wait("vector", sPE, P.val(sPE))
            P.op("vector", lambda e, blk=blk: e.tensor_scalar_mul(
                t1o[0:16, blk * 512:(blk + 1) * 512], ps_pv[0:16, :], 1.0 / 16.0), inc=sVE)
        t1o_ve = P.val(sVE)

        xf_ld = {}
        xbo_ve = {}

        def emit_xf(tt):
            if tt >= 2:
                P.wait("sync", sVE, xbo_ve[tt - 2])
            else:
                P.wait("sync", sPE, pair_done_pe[7])
            P.op("sync", lambda e, tt=tt: e.dma_start(
                out=xf2[tt % 2][:, :], in_=x_r[:, tt, :]), inc=sLd, inck=16)
            xf_ld[tt] = P.val(sLd)

        emit_xf(0)
        emit_xf(1)

        if stage == "C0":
            dummy_out()
            finish_emit()
            return nc

        P.wait("tensor", sCa, wot_ca)
        P.wait("tensor", sVE, t1o_ve)
        o_evac = {}
        st_done = {}
        ntiles = 1 if stage == "C1" else 16
        for tt in range(ntiles):
            P.wait("vector", sLd, max(xf_ld[tt], bo_ld))
            P.op("vector", lambda e, tt=tt: e.tensor_add(
                xbo[:, :], xf2[tt % 2][:, :], bo_sb[:, :]), inc=sVE)
            xbo_ve[tt] = P.val(sVE)
            for dh in range(2):
                fps = ps_s1[:, dh * 512:(dh + 1) * 512]
                ops_ = ps_s0[:, dh * 512:(dh + 1) * 512]
                P.wait("tensor", sVE, o_evac.get((tt - 1, dh), 0))
                for ic in range(8):
                    P.op("tensor", lambda e, ic=ic, tt=tt, dh=dh, fps=fps: e.matmul(
                        fps, qlt[:, ic * 2048 + tt * 128: ic * 2048 + (tt + 1) * 128],
                        qrh[:, ic * 1024 + dh * 512: ic * 1024 + (dh + 1) * 512],
                        start=(ic == 0), stop=(ic == 7)))
                for oc in range(8):
                    P.op("tensor", lambda e, oc=oc, tt=tt, dh=dh, ops_=ops_: e.matmul(
                        ops_, numt[:, oc * 2048 + tt * 128: oc * 2048 + (tt + 1) * 128],
                        wot_v[:, oc * 1024 + dh * 512: oc * 1024 + (dh + 1) * 512],
                        start=(oc == 0), stop=False))
                P.op("tensor", lambda e, tt=tt, dh=dh, ops_=ops_: e.matmul(
                    ops_, t1o[0:16, tt * 128:(tt + 1) * 128],
                    bot_s[0:16, dh * 512:(dh + 1) * 512],
                    start=False, stop=True), inc=sPE)
                mm_pe = P.val(sPE)
                P.wait("vector", sPE, mm_pe)
                P.wait("vector", sVE, xbo_ve[tt])
                P.op("vector", lambda e, tt=tt, dh=dh, fps=fps: e.scalar_tensor_tensor(
                    fscr[:, :], fps, ls_all[:, tt:tt + 1],
                    rs_full[:, dh * 512:(dh + 1) * 512], AL.mult, AL.mult), inc=sVE)
                P.op("vector", lambda e, dh=dh, ops_=ops_: e.scalar_tensor_tensor(
                    oscr[:, :], ops_, 1.0, xbo[:, dh * 512:(dh + 1) * 512],
                    AL.bypass, AL.add), inc=sVE)
                o_evac[(tt, dh)] = P.val(sVE)
                P.wait("vector", sVE, P.val(sVE))
                ob = outst[dh]
                P.wait("vector", sSt, st_done.get((tt - 1, dh), 0))
                P.op("vector", lambda e, ob=ob: e.tensor_add(ob[:, :], fscr[:, :], oscr[:, :]), inc=sVE)
                P.wait("sync", sVE, P.val(sVE))
                P.op("sync", lambda e, tt=tt, dh=dh, ob=ob: e.dma_start(
                    out=out_r[:, tt, dh * 512:(dh + 1) * 512], in_=ob[:, :]), inc=sSt, inck=16)
                st_done[(tt, dh)] = P.val(sSt)
            if tt + 2 < 16:
                emit_xf(tt + 2)

        if stage == "C1":
            dummy_out()
            finish_emit()
            return nc

        P.wait("sync", sSt, P.val(sSt))

        finish_emit()

    return nc


def _stack_a(aq, ak, av):
    a96 = np.zeros((96, D), dtype=np.float32)
    a96[0:16] = aq
    a96[32:48] = ak
    a96[64:80] = av
    return a96.T


def kernel(**inputs):
    x = np.asarray(inputs["L0_attn_x"])
    wq = np.asarray(inputs["L0_attn_Wq"])
    wk = np.asarray(inputs["L0_attn_Wk"])
    wv = np.asarray(inputs["L0_attn_Wv"])
    wo = np.asarray(inputs["L0_attn_Wo"])
    bo = np.asarray(inputs["L0_attn_bo"])
    aq = np.asarray(inputs["L0_attn_Aq"])
    bq = np.asarray(inputs["L0_attn_Bq"])
    ak = np.asarray(inputs["L0_attn_Ak"])
    bk = np.asarray(inputs["L0_attn_Bk"])
    av = np.asarray(inputs["L0_attn_Av"])
    bv = np.asarray(inputs["L0_attn_Bv"])
    ao = np.asarray(inputs["L0_attn_Ao"])
    bo_l = np.asarray(inputs["L0_attn_Bo"])
    rhs = np.asarray(inputs["L0_ffn_rhs"])

    C = np.ascontiguousarray
    shared = {
        "wqt": C(wq.T), "wkt": C(wk.T), "wvt": C(wv.T), "wot": C(wo.T),
        "aqkvt": C(_stack_a(aq, ak, av)), "aot": C(ao.T),
        "bqt": C(bq.T), "bkt": C(bk.T), "bvt": C(bv.T), "bot": C(bo_l.T),
        "rhst": C(rhs.T),
        "bo_full": C(np.broadcast_to(bo, (128, D)).copy()),
    }

    in_maps = []
    for c in range(8):
        b, half = c // 2, c % 2
        own = x[b, half * 2048:(half + 1) * 2048]
        oth = x[b, (1 - half) * 2048:(2 - half) * 2048]
        m = dict(shared)
        m["x"] = C(np.concatenate([own, oth], axis=0))
        in_maps.append(m)

    nc = build_nc()
    res = run_bass_kernel_spmd(nc, in_maps, core_ids=list(range(8)))

    out = np.empty((4, NSEQ, D), dtype=np.float32)
    for c in range(8):
        b, half = c // 2, c % 2
        out[b, half * 2048:(half + 1) * 2048] = res.results[c]["out"]
    return out


# version 34
# speedup vs baseline: 1.0625x; 1.0625x over previous
import numpy as np
import ml_dtypes
import concourse.bass as bass
import concourse.mybir as mybir
from concourse.bass_utils import run_bass_kernel_spmd

F32 = mybir.dt.float32
BF16 = mybir.dt.bfloat16
BF = ml_dtypes.bfloat16
AL = mybir.AluOpType
ACTF = mybir.ActivationFunctionType
AXX = mybir.AxisListType.X

D = 1024
NSEQ = 4096
TOWN = 2048
GROUPS = 8
OWN_GROUPS = 4
PAIRS = 8
KCH = 32
QB = 4
CR = float(np.float32(1.5 * 2 ** 23))
EPS = 1e-5
MHAT = 25.0


class Plan:
    ENGS = ("sync", "gpsimd", "tensor", "vector", "scalar")

    def __init__(self):
        self.q = {e: [] for e in self.ENGS}
        self.n = {}

    def inc(self, sem, k=1):
        self.n[sem] = self.n.get(sem, 0) + k
        return self.n[sem]

    def val(self, sem):
        return self.n.get(sem, 0)

    def add(self, eng, fn):
        self.q[eng].append(fn)

    def wait(self, eng, sem, v):
        if v > 0:
            self.add(eng, lambda e, s=sem, vv=v: e.wait_ge(s, vv))

    def op(self, eng, build, inc=None, inck=1):
        if inc is not None:
            self.inc(inc, inck)
            self.add(eng, lambda e, b=build, s=inc, k=inck: b(e).then_inc(s, k))
        else:
            self.add(eng, lambda e, b=build: b(e))


def build_nc(stage="full"):
    nc = bass.Bass("TRN2", target_bir_lowering=False, debug=False)

    x_d = nc.dram_tensor("x", [NSEQ, D], F32, kind="ExternalInput").ap()
    wqt_d = nc.dram_tensor("wqt", [D, D], F32, kind="ExternalInput").ap()
    wkt_d = nc.dram_tensor("wkt", [D, D], F32, kind="ExternalInput").ap()
    wvt_d = nc.dram_tensor("wvt", [D, D], F32, kind="ExternalInput").ap()
    wot_d = nc.dram_tensor("wot", [D, D], F32, kind="ExternalInput").ap()
    aqkv_d = nc.dram_tensor("aqkvt", [D, 96], F32, kind="ExternalInput").ap()
    aot_d = nc.dram_tensor("aot", [D, 16], F32, kind="ExternalInput").ap()
    bqt_d = nc.dram_tensor("bqt", [16, D], F32, kind="ExternalInput").ap()
    bkt_d = nc.dram_tensor("bkt", [16, D], F32, kind="ExternalInput").ap()
    bvt_d = nc.dram_tensor("bvt", [16, D], F32, kind="ExternalInput").ap()
    bot_d = nc.dram_tensor("bot", [16, D], F32, kind="ExternalInput").ap()
    rhst_d = nc.dram_tensor("rhst", [D, D], F32, kind="ExternalInput").ap()
    bof_d = nc.dram_tensor("bo_full", [128, D], F32, kind="ExternalInput").ap()
    out_d = nc.dram_tensor("out", [TOWN, D], F32, kind="ExternalOutput").ap()

    kt_h = nc.dram_tensor("kt_h", [D, NSEQ], BF16, kind="Internal").ap()
    v_h = nc.dram_tensor("v_h", [NSEQ, D], BF16, kind="Internal").ap()

    ident_d = nc.inline_tensor(np.eye(128, dtype=np.float32), name="ident")

    x_r = x_d.rearrange("(p a) d -> p a d", a=32)
    v_h_r = v_h.rearrange("(a p) o -> p a o", p=128)
    out_r = out_d.rearrange("(a p) d -> p a d", p=128)

    UW = 27648

    from contextlib import ExitStack
    ctx = ExitStack()
    sb = lambda *a: ctx.enter_context(nc.sbuf_tensor(*a))
    pst = lambda *a: ctx.enter_context(nc.psum_tensor(*a))
    sem = lambda n: ctx.enter_context(nc.semaphore(n))
    with ctx:
        qt = sb("qt", [128, 8 * 2048], BF16)
        qlt = sb("qlt", [128, 8 * 2048], BF16)
        qrh = sb("qrh", [128, 8 * 1024], BF16)
        rs_full = sb("rs_full", [128, 1024], F32)
        ls_all = sb("ls_all", [128, 16], F32)
        stats = sb("stats", [128, 160], F32)
        ones_b = sb("ones_b", [128, 64], BF16)
        ones_f = sb("ones_f", [128, 128], F32)
        ident_f = sb("ident_f", [128, 128], F32)
        biasx = sb("biasx", [128, 4], F32)
        aqkv_s = sb("aqkv_s", [128, 768], BF16)
        aot_s = sb("aot_s", [128, 128], BF16)
        bqt_s = sb("bqt_s", [128, 1024], BF16)
        bkt_s = sb("bkt_s", [128, 1024], BF16)
        bvt_s = sb("bvt_s", [128, 1024], BF16)
        bot_s = sb("bot_s", [128, 1024], BF16)
        t1s = sb("t1s", [128, 512], BF16)
        U = sb("U", [128, UW], F32)
        ps_s0 = pst("ps_s0", [128, 1024], F32)
        ps_s1 = pst("ps_s1", [128, 1024], F32)
        ps_s2 = pst("ps_s2", [128, 1024], F32)
        ps_pv = pst("ps_pv", [128, 512], F32)
        ps_l = pst("ps_l", [128, 512], F32)
        sLd = sem("sLd")
        sCa = sem("sCa")
        sSt = sem("sSt")
        sPE = sem("sPE")
        sVE = sem("sVE")
        sAC = sem("sAC")
        block = ctx.enter_context(nc.Block())

        P = Plan()

        def uf(off, n):
            ap = U[:, off:off + n]
            return ap

        def ui(off, n):
            ap = U[:, off:off + n].bitcast(mybir.dt.int32)
            return ap

        def ub(off, nbf):
            ap = U[:, off:off + nbf // 2].bitcast(BF16)
            assert tuple(ap.shape) == (128, nbf), ap.shape
            return ap

        P.op("sync", lambda e: e.dma_start(out=ident_f[:, :], in_=ident_d.ap()), inc=sLd, inck=16)
        P.op("vector", lambda e: e.memset(biasx[:, :], -MHAT))
        P.op("vector", lambda e: e.memset(ones_b[:, :], 1.0))
        P.op("vector", lambda e: e.memset(ones_f[:, :], 1.0), inc=sVE)
        init_ve = P.val(sVE)
        P.wait("scalar", sVE, init_ve)
        P.op("scalar", lambda e: e.activation(
            stats[:, 156:157], ones_f[:, 0:1], ACTF.Sqrt), inc=sAC)
        rhst_v = uf(4096, 8192)
        rhst_state = {}

        wq_v = ub(0, 8192)
        wk_v = ub(4096, 8192)
        wv_v = ub(8192, 8192)
        xg = [uf(12288, 4096), uf(16384, 4096)]
        xnT2 = [ub(20480, 4096), ub(22528, 4096)]
        qs2b = [uf(24576, 1024), uf(25600, 1024)]
        kvst = [ub(26624, 512), ub(26880, 512), ub(27136, 512), ub(27392, 512)]

        wload = {}

        def emit_lora_loads():
            P.op("gpsimd", lambda e: e.dma_start(
                out=aqkv_s[:, :].rearrange("p (c r) -> p c r", c=8),
                in_=aqkv_d.rearrange("(c p) r -> p c r", p=128)), inc=sCa, inck=16)
            P.op("gpsimd", lambda e: e.dma_start(
                out=aot_s[:, :].rearrange("p (c r) -> p c r", c=8),
                in_=aot_d.rearrange("(c p) r -> p c r", p=128)), inc=sCa, inck=16)
            for dst, src_, r0 in ((bqt_s, bqt_d, 0), (bkt_s, bkt_d, 32),
                                  (bvt_s, bvt_d, 64), (bot_s, bot_d, 0)):
                P.op("gpsimd", lambda e, d=dst, s=src_, r0=r0: e.dma_start(
                    out=d[r0:r0 + 16, :], in_=s), inc=sCa, inck=16)
            return P.val(sCa)

        def emit_heavy_wloads():
            for nm, dst, src_, tpj in (("k", wk_v, wkt_d, 3), ("v", wv_v, wvt_d, 7)):
                P.wait("gpsimd", sPE, a1_tp[tpj])
                P.op("gpsimd", lambda e, d=dst, s=src_: e.dma_start(
                    out=d.rearrange("p (c o) -> p c o", c=8),
                    in_=s.rearrange("(c p) o -> p c o", p=128)), inc=sCa, inck=16)
                wload[nm] = P.val(sCa)

        xload_ld = {}
        xload_ca = {}
        xg_free = {}

        def emit_xload(g):
            if g >= 2:
                pe_v, ve_v = xg_free[g - 2]
                P.wait("sync", sPE, pe_v)
                P.wait("sync", sVE, ve_v)
            if g == 0:
                P.op("sync", lambda e, g=g: e.dma_start(
                    out=xg[g % 2][:, 0:2048].rearrange("p (a d) -> p a d", a=2),
                    in_=x_r[:, 4 * g:4 * g + 2, :]), inc=sLd, inck=16)
                P.op("gpsimd", lambda e, g=g: e.dma_start(
                    out=xg[g % 2][:, 2048:4096].rearrange("p (a d) -> p a d", a=2),
                    in_=x_r[:, 4 * g + 2:4 * g + 4, :]), inc=sCa, inck=16)
                xload_ca[g] = P.val(sCa)
            else:
                P.op("sync", lambda e, g=g: e.dma_start(
                    out=xg[g % 2].rearrange("p (a d) -> p a d", a=4),
                    in_=x_r[:, 4 * g:4 * g + 4, :]), inc=sLd, inck=16)
            xload_ld[g] = P.val(sLd)

        emit_xload(0)
        rhst_r = rhst_d.rearrange("(c p) i -> p c i", p=128)
        P.op("gpsimd", lambda e: e.dma_start(
            out=rhst_v[:, 0:4096].rearrange("p (c i) -> p c i", c=4),
            in_=rhst_r[:, 0:4, :]), inc=sCa, inck=16)
        rhst_state["ca"] = P.val(sCa)
        P.op("sync", lambda e: e.dma_start(
            out=rhst_v[:, 4096:8192].rearrange("p (c i) -> p c i", c=4),
            in_=rhst_r[:, 4:8, :]), inc=sLd, inck=16)
        rhst_state["ld"] = P.val(sLd)
        emit_xload(1)
        lora_ca = emit_lora_loads()
        P.op("gpsimd", lambda e: e.dma_start(
            out=wq_v.rearrange("p (c o) -> p c o", c=8),
            in_=wqt_d.rearrange("(c p) o -> p c o", p=128)), inc=sCa, inck=16)
        wload["q"] = P.val(sCa)

        ps_slots = [ps_s0[:, 0:512], ps_s0[:, 512:1024], ps_s1[:, 0:512], ps_s1[:, 512:1024]]
        slot_evac = [0, 0, 0, 0]
        slot_i = [0]
        stage_hist = []
        k_st = {}
        v_st = {}
        s2_free_ac = [0]

        def proj_tile(mms, evac_fn, fold_gate=None):
            si = slot_i[0] % 4
            slot_i[0] += 1
            ps = ps_slots[si]
            P.wait("tensor", sAC, slot_evac[si])
            n = len(mms)
            for i, mm in enumerate(mms):
                if fold_gate is not None and i == n - 1:
                    P.wait("tensor", fold_gate[0], fold_gate[1])
                lh, rh = mm[0], mm[1]
                tp = mm[2] if len(mm) > 2 else None
                P.op("tensor", lambda e, lh=lh, rh=rh, i=i, ps=ps, n=n, tp=tp:
                     e.matmul(ps, lh, rh, start=(i == 0), stop=(i == n - 1),
                              tile_position=tp),
                     inc=(sPE if i == n - 1 else None))
            P.wait("scalar", sPE, P.val(sPE))
            P.op("scalar", lambda e, ps=ps: evac_fn(e, ps), inc=sAC)
            slot_evac[si] = P.val(sAC)

        last_t1 = [0]

        def lora_t1(xnT, ready_ac):
            P.wait("tensor", sCa, lora_ca)
            P.wait("tensor", sAC, ready_ac)
            P.wait("tensor", sAC, last_t1[0])
            for ic in range(8):
                P.op("tensor", lambda e, ic=ic, xnT=xnT: e.matmul(
                    ps_pv[0:96, :], aqkv_s[:, ic * 96:(ic + 1) * 96],
                    xnT[:, ic * 512:(ic + 1) * 512],
                    start=(ic == 0), stop=(ic == 7)), inc=(sPE if ic == 7 else None))
            P.wait("scalar", sPE, P.val(sPE))
            P.op("scalar", lambda e: e.mul(t1s[0:96, :], ps_pv[0:96, :], 1.0 / 16.0), inc=sAC)
            last_t1[0] = P.val(sAC)
            return P.val(sAC)

        def kv_stage_store():
            idx = len(stage_hist)
            st = kvst[idx % 4]
            wait_st = stage_hist[idx - 4] if idx >= 4 else 0
            P.wait("scalar", sSt, wait_st)

            def evac(e, ps, st=st):
                return e.copy(st[:, :], ps)
            return st, evac

        s2_half = [0, 0]
        s2_par = [0]

        def transpose_batch(src_ap, dst_fn, in_ve):
            P.wait("tensor", sVE, in_ve)
            tp = 0
            for h in range(2):
                par = s2_par[0]
                s2_par[0] ^= 1
                P.wait("tensor", sAC, s2_half[par])
                for ic in range(4):
                    icg = h * 4 + ic
                    P.op("tensor", lambda e, icg=icg, ic=ic, s=src_ap, par=par: e.transpose(
                        ps_s2[:, par * 512 + ic * 128: par * 512 + (ic + 1) * 128],
                        s[:, icg * 128:(icg + 1) * 128],
                        ident_f[:, :]), inc=(sPE if ic == 3 else None))
                tp = P.val(sPE)
                P.wait("scalar", sPE, tp)
                P.op("scalar", lambda e, h=h, par=par: dst_fn(
                    e, h, ps_s2[:, par * 512:(par + 1) * 512]), inc=sAC)
                s2_half[par] = P.val(sAC)
            return tp

        a1_state = {}
        a1_tp = {}

        def emit_a1():
            for hh in range(2):
                if hh == 0:
                    P.wait("vector", sCa, rhst_state["ca"])
                else:
                    P.wait("vector", sLd, rhst_state["ld"])
                c0 = hh * 4
                P.op("vector", lambda e, c0=c0, hh=hh: e.reduce_max(
                    stats[:, c0:c0 + 4],
                    rhst_v[:, hh * 4096:(hh + 1) * 4096].rearrange("p (c i) -> p c i", c=4),
                    axis=AXX, apply_absolute_value=True), inc=sVE)
                P.wait("vector", sVE, P.val(sVE))
                P.op("vector", lambda e, c0=c0: e.tensor_scalar_max(
                    stats[:, 8 + c0:8 + c0 + 4], stats[:, c0:c0 + 4], 1e-12), inc=sVE)
                P.wait("vector", sVE, P.val(sVE))
                P.op("vector", lambda e, c0=c0: e.reciprocal(
                    stats[:, 16 + c0:16 + c0 + 4], stats[:, 8 + c0:8 + c0 + 4]), inc=sVE)
                P.wait("vector", sVE, P.val(sVE))
                P.op("vector", lambda e, c0=c0: e.tensor_scalar_mul(
                    stats[:, 24 + c0:24 + c0 + 4], stats[:, 16 + c0:16 + c0 + 4], 127.0))
                P.op("vector", lambda e, c0=c0: e.tensor_scalar_mul(
                    stats[:, 32 + c0:32 + c0 + 4], stats[:, 8 + c0:8 + c0 + 4],
                    1.0 / 127.0), inc=sVE)
                a1_state["sc_ve%d" % hh] = P.val(sVE)
                P.wait("vector", sVE, P.val(sVE))

            def emit_j(j):
                sl = rhst_v[:, j * 1024:(j + 1) * 1024]
                sli = sl.bitcast(mybir.dt.int32)
                P.op("vector", lambda e, j=j, sli=sli, sl=sl: e.tensor_scalar_mul(
                    sli, sl, stats[:, 24 + j:25 + j]), inc=sVE)
                P.wait("vector", sVE, P.val(sVE))
                P.op("vector", lambda e, sl=sl, sli=sli: e.tensor_copy(sl, sli), inc=sVE)
                qv = P.val(sVE)

                def evac_rh(e, h, psh, j=j):
                    return e.copy(
                        qrh.rearrange("p (c o) -> p c o", c=8)[:, 4 * h:4 * h + 4,
                                                               j * 128:(j + 1) * 128],
                        psh.rearrange("p (c t) -> p c t", c=4))
                a1_tp[j] = transpose_batch(sl, evac_rh, qv)

            rs_row = uf(26624, 1024)
            rst8 = uf(26752, 128)
            for j in range(2):
                emit_j(j)
            par = s2_par[0]
            s2_par[0] ^= 1
            P.wait("tensor", sAC, s2_half[par])
            P.wait("tensor", sVE, a1_state["sc_ve1"])
            P.op("tensor", lambda e, par=par: e.transpose(
                ps_s2[0:8, par * 512:par * 512 + 128], stats[:, 32:40],
                ident_f[:, :]), inc=sPE)
            rs_tp = P.val(sPE)
            P.wait("scalar", sPE, rs_tp)
            P.op("scalar", lambda e, par=par: e.copy(
                rst8[0:8, 0:128], ps_s2[0:8, par * 512:par * 512 + 128]), inc=sAC)
            s2_half[par] = P.val(sAC)
            P.wait("sync", sAC, P.val(sAC))
            for j in range(8):
                P.op("sync", lambda e, j=j: e.dma_start(
                    out=rs_row[0:1, j * 128:(j + 1) * 128], in_=rst8[j:j + 1, 0:128]),
                    inc=sSt, inck=16)
            rs_st = P.val(sSt)
            for j in range(2, 4):
                emit_j(j)
            P.wait("tensor", sSt, rs_st)
            P.wait("tensor", sAC, s2_half[0])
            P.wait("tensor", sAC, s2_half[1])
            P.op("tensor", lambda e: e.matmul(ps_s2[:, 0:512], ones_f[0:1, 0:128],
                                              rs_row[0:1, 0:512], start=True, stop=True))
            P.op("tensor", lambda e: e.matmul(ps_s2[:, 512:1024], ones_f[0:1, 0:128],
                                              rs_row[0:1, 512:1024], start=True, stop=True), inc=sPE)
            P.wait("scalar", sPE, P.val(sPE))
            P.op("scalar", lambda e: e.copy(rs_full[:, :], ps_s2[:, :]), inc=sAC)
            s2_half[0] = P.val(sAC)
            s2_half[1] = P.val(sAC)
            for j in range(4, 8):
                emit_j(j)

        qtrans_pe = {}
        qs2b_guard = [0]

        def sumsq(tt, xgv, b0):
            for hv in range(2):
                P.op("vector", lambda e, tt=tt, hv=hv, xgv=xgv, b0=b0: e.scalar_tensor_tensor(
                    ps_l[:, :], xgv[:, tt * 1024 + hv * 512:tt * 1024 + (hv + 1) * 512],
                    1.0, xgv[:, tt * 1024 + hv * 512:tt * 1024 + (hv + 1) * 512],
                    AL.bypass, AL.mult,
                    accum_out=stats[:, b0 + 48 + 2 * tt + hv:b0 + 49 + 2 * tt + hv]))

        def fe_stats(g):
            b0 = 40 + 56 * (g % 2)
            xgv = xg[g % 2]
            if g in xload_ca:
                P.wait("vector", sLd, xload_ld[g])
                for tt in range(2):
                    sumsq(tt, xgv, b0)
                P.op("vector", lambda e, xgv=xgv, b0=b0: e.reduce_sum(
                    stats[:, b0 + 4:b0 + 6],
                    xgv[:, 0:2048].rearrange("p (a d) -> p a d", a=2), axis=AXX))
                P.wait("vector", sCa, xload_ca[g])
                for tt in range(2, 4):
                    sumsq(tt, xgv, b0)
                P.op("vector", lambda e, xgv=xgv, b0=b0: e.reduce_sum(
                    stats[:, b0 + 6:b0 + 8],
                    xgv[:, 2048:4096].rearrange("p (a d) -> p a d", a=2), axis=AXX), inc=sVE)
            else:
                P.wait("vector", sLd, xload_ld[g])
                for tt in range(4):
                    sumsq(tt, xgv, b0)
                P.op("vector", lambda e, xgv=xgv, b0=b0: e.reduce_sum(
                    stats[:, b0 + 4:b0 + 8], xgv.rearrange("p (a d) -> p a d", a=4), axis=AXX), inc=sVE)
            P.wait("vector", sVE, P.val(sVE))
            P.op("vector", lambda e, b0=b0: e.tensor_add(
                stats[:, b0:b0 + 4], stats[:, b0 + 48:b0 + 56:2],
                stats[:, b0 + 49:b0 + 56:2]), inc=sVE)
            P.wait("vector", sVE, P.val(sVE))
            P.op("vector", lambda e, b0=b0: e.tensor_scalar_mul(
                stats[:, b0 + 8:b0 + 12], stats[:, b0 + 4:b0 + 8], 1.0 / D))
            P.op("vector", lambda e, b0=b0: e.tensor_scalar(
                stats[:, b0 + 12:b0 + 16], stats[:, b0:b0 + 4], 1.0 / D, EPS,
                AL.mult, AL.add), inc=sVE)
            P.wait("vector", sVE, P.val(sVE))
            P.op("vector", lambda e, b0=b0: e.tensor_mul(
                stats[:, b0 + 16:b0 + 20], stats[:, b0 + 8:b0 + 12], stats[:, b0 + 8:b0 + 12]), inc=sVE)
            P.wait("vector", sVE, P.val(sVE))
            P.op("vector", lambda e, b0=b0: e.tensor_sub(
                stats[:, b0 + 20:b0 + 24], stats[:, b0 + 12:b0 + 16], stats[:, b0 + 16:b0 + 20]), inc=sVE)
            P.wait("vector", sVE, P.val(sVE))
            P.op("vector", lambda e, b0=b0: e.reciprocal(
                stats[:, b0 + 24:b0 + 28], stats[:, b0 + 20:b0 + 24]), inc=sVE)
            P.wait("scalar", sVE, P.val(sVE))
            P.op("scalar", lambda e, b0=b0: e.activation(
                stats[:, b0 + 28:b0 + 32], stats[:, b0 + 24:b0 + 28], ACTF.Sqrt), inc=sAC)
            P.wait("vector", sAC, P.val(sAC))
            P.op("vector", lambda e, b0=b0: e.tensor_mul(
                stats[:, b0 + 32:b0 + 36], stats[:, b0 + 28:b0 + 32], stats[:, b0 + 28:b0 + 32]), inc=sVE)
            P.wait("vector", sVE, P.val(sVE))
            P.op("vector", lambda e, b0=b0: e.tensor_mul(
                stats[:, b0 + 36:b0 + 40], stats[:, b0 + 32:b0 + 36], stats[:, b0 + 20:b0 + 24]), inc=sVE)
            P.wait("vector", sVE, P.val(sVE))
            P.op("vector", lambda e, b0=b0: e.tensor_scalar(
                stats[:, b0 + 40:b0 + 44], stats[:, b0 + 36:b0 + 40], -0.5, 1.5,
                AL.mult, AL.add), inc=sVE)
            P.wait("vector", sVE, P.val(sVE))
            P.op("vector", lambda e, b0=b0: e.tensor_mul(
                stats[:, b0 + 44:b0 + 48], stats[:, b0 + 28:b0 + 32], stats[:, b0 + 40:b0 + 44]), inc=sVE)
            P.wait("vector", sVE, P.val(sVE))

        fe_lasttp = {}

        def frontend_xnT(g):
            b0 = 40 + 56 * (g % 2)
            xnT = xnT2[g % 2]
            xgv = xg[g % 2]
            xn_ve = {}
            for tt in range(4):
                P.op("vector", lambda e, tt=tt, xgv=xgv, b0=b0: e.tensor_scalar(
                    xgv[:, tt * 1024:(tt + 1) * 1024], xgv[:, tt * 1024:(tt + 1) * 1024],
                    stats[:, b0 + 8 + tt:b0 + 9 + tt], stats[:, b0 + 44 + tt:b0 + 45 + tt],
                    AL.subtract, AL.mult), inc=sVE)
                xn_ve[tt] = P.val(sVE)

            last_tp = 0
            for tt in range(4):
                def evac_xn(e, h, psh, tt=tt, xnT=xnT):
                    return e.copy(
                        xnT.rearrange("p (c t) -> p c t", c=8)[:, 4 * h:4 * h + 4,
                                                               tt * 128:(tt + 1) * 128],
                        psh.rearrange("p (c t) -> p c t", c=4))
                last_tp = transpose_batch(xgv[:, tt * 1024:(tt + 1) * 1024], evac_xn, xn_ve[tt])
            xnT_ac_of[g] = P.val(sAC)
            fe_lasttp[g] = last_tp

        def frontend_quant(g):
            own = g < OWN_GROUPS
            xgv = xg[g % 2]
            quant_ve = 0
            if own:
                for tt in range(4):
                    gt = g * 4 + tt
                    qsb = qs2b[tt % 2]
                    if tt >= 2:
                        P.wait("vector", sPE, qtrans_pe[(g, tt - 2)])
                    P.op("vector", lambda e, tt=tt, xgv=xgv: e.reduce_max(
                        stats[:, 152:153], xgv[:, tt * 1024:(tt + 1) * 1024], axis=AXX,
                        apply_absolute_value=True), inc=sVE)
                    P.wait("vector", sVE, P.val(sVE))
                    P.op("vector", lambda e: e.tensor_scalar_max(
                        stats[:, 153:154], stats[:, 152:153], 1e-12), inc=sVE)
                    P.wait("vector", sVE, P.val(sVE))
                    P.op("vector", lambda e: e.reciprocal(stats[:, 154:155], stats[:, 153:154]), inc=sVE)
                    P.wait("vector", sVE, P.val(sVE))
                    P.op("vector", lambda e: e.tensor_scalar_mul(stats[:, 155:156], stats[:, 154:155], 127.0))
                    P.op("vector", lambda e, gt=gt: e.tensor_scalar_mul(
                        ls_all[:, gt:gt + 1], stats[:, 153:154], 1.0 / 127.0), inc=sVE)
                    P.wait("vector", sVE, P.val(sVE))
                    qsbi = qsb.bitcast(mybir.dt.int32)
                    P.op("vector", lambda e, tt=tt, xgv=xgv, qsbi=qsbi: e.tensor_scalar_mul(
                        qsbi, xgv[:, tt * 1024:(tt + 1) * 1024],
                        stats[:, 155:156]), inc=sVE)
                    P.wait("vector", sVE, P.val(sVE))
                    P.op("vector", lambda e, qsb=qsb, qsbi=qsbi: e.tensor_copy(qsb[:, :], qsbi), inc=sVE)
                    qv = P.val(sVE)

                    def evac_q(e, h, psh, gt=gt):
                        return e.copy(
                            qlt.rearrange("p (c t) -> p c t", c=8)[:, 4 * h:4 * h + 4,
                                                                   gt * 128:(gt + 1) * 128],
                            psh.rearrange("p (c t) -> p c t", c=4))
                    qtrans_pe[(g, tt)] = transpose_batch(qsb, evac_q, qv)
                qs2b_guard[0] = qtrans_pe[(g, 3)]
                quant_ve = P.val(sVE)
            xg_free[g] = (qtrans_pe[(g, 3)] if own else fe_lasttp[g], quant_ve)
            if g + 2 < GROUPS:
                emit_xload(g + 2)

        def frontend(g):
            frontend_xnT(g)
            frontend_quant(g)


        def backend(g, t1_ac):
            own = g < OWN_GROUPS
            xnT = xnT2[g % 2]
            P.wait("tensor", sCa, wload["k"])
            for oc in range(8):
                mms = [(wk_v[:, ic * 1024 + oc * 128: ic * 1024 + (oc + 1) * 128],
                        xnT[:, ic * 512:(ic + 1) * 512]) for ic in range(8)]
                mms.append((bkt_s[32:48, oc * 128:(oc + 1) * 128], t1s[32:48, :], (32, 0)))
                st, evac = kv_stage_store()
                proj_tile(mms, evac, fold_gate=(sAC, t1_ac) if oc == 0 else None)
                P.wait("sync", sAC, P.val(sAC))
                P.op("sync", lambda e, oc=oc, g=g, st=st: e.dma_start(
                    out=kt_h[oc * 128:(oc + 1) * 128, g * 512:(g + 1) * 512],
                    in_=st[:, :]), inc=sSt, inck=16)
                stage_hist.append(P.val(sSt))
                k_st[(g, oc)] = P.val(sSt)

            P.wait("tensor", sCa, wload["v"])
            for tt in range(4):
                for oh in range(2):
                    mms = [(xnT[:, ic * 512 + tt * 128: ic * 512 + (tt + 1) * 128],
                            wv_v[:, ic * 1024 + oh * 512: ic * 1024 + (oh + 1) * 512])
                           for ic in range(8)]
                    mms.append((t1s[64:80, tt * 128:(tt + 1) * 128],
                                bvt_s[64:80, oh * 512:(oh + 1) * 512], (64, 0)))
                    st, evac = kv_stage_store()
                    proj_tile(mms, evac)
                    P.wait("sync", sAC, P.val(sAC))
                    P.op("sync", lambda e, g=g, tt=tt, oh=oh, st=st: e.dma_start(
                        out=v_h_r[:, g * 4 + tt, oh * 512:(oh + 1) * 512],
                        in_=st[:, :]), inc=sSt, inck=16)
                    stage_hist.append(P.val(sSt))
                    v_st[(g, tt, oh)] = P.val(sSt)

            if own:
                P.wait("tensor", sCa, wload["q"])
                for oc in range(8):
                    mms = [(wq_v[:, ic * 1024 + oc * 128: ic * 1024 + (oc + 1) * 128],
                            xnT[:, ic * 512:(ic + 1) * 512]) for ic in range(8)]
                    mms.append((bqt_s[0:16, oc * 128:(oc + 1) * 128], t1s[0:16, :]))

                    def evac_q2(e, ps, oc=oc, g=g):
                        return e.copy(
                            qt[:, oc * 2048 + g * 512: oc * 2048 + (g + 1) * 512], ps)
                    proj_tile(mms, evac_q2)


        xnT_ac_of = {}
        fe_stats(0)
        frontend_xnT(0)
        emit_a1()
        emit_heavy_wloads()
        fe_stats(1)
        frontend_quant(0)
        for g in range(GROUPS):
            t1_ac = lora_t1(xnT2[g % 2], xnT_ac_of[g])
            if g + 1 < GROUPS:
                frontend(g + 1)
            if g + 2 < GROUPS:
                fe_stats(g + 2)
            backend(g, t1_ac)

        a2_st = P.val(sSt)
        a2_ve = P.val(sVE)
        a2_ac = P.val(sAC)

        def dummy_out():
            P.wait("sync", sVE, P.val(sVE))
            P.wait("sync", sPE, P.val(sPE))
            P.wait("sync", sAC, P.val(sAC))
            for tt in range(16):
                P.op("sync", lambda e, tt=tt: e.dma_start(
                    out=out_r[:, tt, :], in_=bo_sb[:, :]), inc=sSt, inck=16)
            P.wait("sync", sSt, P.val(sSt))

        def finish_emit():
            @block.sync
            def _(e):
                for fn in P.q["sync"]:
                    fn(e)

            @block.gpsimd
            def _(e):
                for fn in P.q["gpsimd"]:
                    fn(e)

            @block.tensor
            def _(e):
                for fn in P.q["tensor"]:
                    fn(e)

            @block.vector
            def _(e):
                for fn in P.q["vector"]:
                    fn(e)

            @block.scalar
            def _(e):
                for fn in P.q["scalar"]:
                    fn(e)

        if stage == "A":
            dummy_out()
            finish_emit()
            return nc

        ktb = [ub(0, 4096), ub(2048, 4096)]
        vsb = [ub(4096, 4096), ub(6144, 4096)]
        pbuf = [ub(8192, 1024), ub(8704, 1024), ub(9216, 1024)]
        l_sp = uf(10240, 512)
        rl_full = uf(10752, 512)
        numt = ub(11264, 16384)

        kv_load_ld = {}
        kt_ld = {}

        def emit_kv_load(p, after_pe):
            P.wait("gpsimd", sPE, after_pe)
            P.wait("gpsimd", sSt, k_st[(GROUPS - 1, p)])
            P.op("gpsimd", lambda e, p=p: e.dma_start(
                out=ktb[p % 2][:, :], in_=kt_h[p * 128:(p + 1) * 128, :]), inc=sCa, inck=16)
            kt_ld[p] = P.val(sCa)
            P.wait("gpsimd", sSt, v_st[(GROUPS - 1, 3, p // 4)])
            P.op("gpsimd", lambda e, p=p: e.dma_start(
                out=vsb[p % 2][:, :].rearrange("p (a o) -> p a o", a=32),
                in_=v_h_r[:, :, p * 128:(p + 1) * 128]), inc=sCa, inck=16)
            kv_load_ld[p] = P.val(sCa)

        emit_kv_load(0, 0)
        emit_kv_load(1, 0)

        P.wait("tensor", sVE, a2_ve)
        P.wait("tensor", sAC, a2_ac)
        P.wait("scalar", sVE, init_ve)

        pair_done_pe = {}
        sring = [ps_s0, ps_s1]
        PV_banks = [ps_pv, ps_s2[:, 0:512]]
        L_banks = [ps_l, ps_s2[:, 512:1024]]
        exp_of = {}
        pcons = {}
        scores_pe = {}
        dance_done = [0, 0]
        prev_repl_pe = [0]
        pending = [None]

        def flush_dance():
            Lb, PVb, pi, p_, qb_, l_ve = pending[0]
            pending[0] = None
            P.wait("tensor", sVE, l_ve)
            P.op("tensor", lambda e, Lb=Lb: e.matmul(
                Lb[0:64, :], ones_f[0:1, 0:64], l_sp[0:1, :],
                start=True, stop=True, tile_position=(0, 0)))
            P.op("tensor", lambda e, Lb=Lb: e.matmul(
                Lb[64:128, :], ones_f[32:33, 0:64], l_sp[32:33, :],
                start=True, stop=True, tile_position=(32, 64)), inc=sPE)
            prev_repl_pe[0] = P.val(sPE)
            P.wait("vector", sPE, prev_repl_pe[0])
            P.op("vector", lambda e, Lb=Lb: e.reciprocal(rl_full[:, :], Lb[:, :]), inc=sVE)
            P.wait("vector", sVE, P.val(sVE))
            P.op("vector", lambda e, p_=p_, qb_=qb_, PVb=PVb: e.tensor_mul(
                numt[:, p_ * 2048 + qb_ * 512: p_ * 2048 + (qb_ + 1) * 512],
                PVb[:, :], rl_full[:, :]), inc=sVE)
            dance_done[pi] = P.val(sVE)

        NCH = QB * KCH
        TOT = PAIRS * NCH

        def emit_scores(c):
            p_, j_ = divmod(c, NCH)
            qb_, kc_ = divmod(j_, KCH)
            if j_ == 0:
                P.wait("tensor", sCa, kt_ld[p_])
            kb = ktb[p_ % 2]
            S = sring[c % 2]
            if c >= 2 and (j_ < 2):
                P.wait("tensor", sAC, exp_of[c - 2])
            qa = qt[0:64, p_ * 2048 + qb_ * 512: p_ * 2048 + (qb_ + 1) * 512]
            qbb = qt[64:128, p_ * 2048 + qb_ * 512: p_ * 2048 + (qb_ + 1) * 512]
            P.op("tensor", lambda e, kc=kc_, qa=qa, S=S, kb=kb: e.matmul(
                S[:, 0:512], kb[0:64, kc * 128:(kc + 1) * 128], qa,
                start=True, stop=True, tile_position=(0, 0)))
            P.op("tensor", lambda e, kc=kc_, qbb=qbb, S=S, kb=kb: e.matmul(
                S[:, 512:1024], kb[64:128, kc * 128:(kc + 1) * 128], qbb,
                start=True, stop=True, tile_position=(64, 0)), inc=sPE)
            scores_pe[c] = P.val(sPE)

        emit_scores(0)
        emit_scores(1)
        for c in range(TOT):
            p, j = divmod(c, NCH)
            qb, kc = divmod(j, KCH)
            pi = qb % 2
            S = sring[c % 2]
            PVb = PV_banks[pi]
            Lb = L_banks[pi]
            vb = vsb[p % 2]
            P.wait("scalar", sPE, scores_pe[c])
            pb = pbuf[c % 3]
            P.op("scalar", lambda e, S=S, pb=pb: e.activation(
                pb[:, :], S[:, :], ACTF.Exp, bias=biasx[:, 0:1], scale=0.125), inc=sAC)
            exp_of[c] = P.val(sAC)
            P.wait("tensor", sAC, exp_of[c])
            if c + 2 < TOT:
                emit_scores(c + 2)
            st = kc == 0
            sp = kc == KCH - 1
            if j == 0:
                P.wait("tensor", sCa, kv_load_ld[p])
            if st:
                P.wait("tensor", sVE, dance_done[pi])
            P.op("tensor", lambda e, vb=vb, kc=kc, pb=pb, st=st, sp=sp, PVb=PVb: e.matmul(
                PVb[0:64, :], vb[:, kc * 128:kc * 128 + 64], pb[:, 0:512],
                start=st, stop=sp, tile_position=(0, 0)))
            P.op("tensor", lambda e, vb=vb, kc=kc, pb=pb, st=st, sp=sp, PVb=PVb: e.matmul(
                PVb[64:128, :], vb[:, kc * 128 + 64:kc * 128 + 128], pb[:, 512:1024],
                start=st, stop=sp, tile_position=(0, 64)))
            P.op("tensor", lambda e, pb=pb, st=st, sp=sp, Lb=Lb: e.matmul(
                Lb[0:1, :], ones_b[:, 0:1], pb[:, 0:512],
                start=st, stop=sp, tile_position=(0, 0)))
            P.op("tensor", lambda e, pb=pb, st=st, sp=sp, Lb=Lb: e.matmul(
                Lb[32:33, :], ones_b[:, 0:1], pb[:, 512:1024],
                start=st, stop=sp, tile_position=(0, 32)), inc=sPE)
            pcons[c] = P.val(sPE)
            if kc == 4 and pending[0] is not None:
                flush_dance()
            if kc == KCH - 1:
                P.wait("vector", sPE, pcons[c])
                P.wait("vector", sPE, prev_repl_pe[0])
                P.op("vector", lambda e, Lb=Lb: e.tensor_copy(l_sp[0:1, :], Lb[0:1, :]))
                P.op("vector", lambda e, Lb=Lb: e.tensor_copy(l_sp[32:33, :], Lb[32:33, :]), inc=sVE)
                pending[0] = (Lb, PVb, pi, p, qb, P.val(sVE))
            if j == NCH - 1:
                pair_done_pe[p] = P.val(sPE)
                if p == PAIRS - 1 and pending[0] is not None:
                    flush_dance()
                if p + 2 < PAIRS:
                    emit_kv_load(p + 2, pair_done_pe[p])
        b_ve = P.val(sVE)

        if stage == "B":
            dummy_out()
            finish_emit()
            return nc

        wot_v = ub(0, 8192)
        t1o = ub(4096, 2048)
        bo_sb = uf(19456, 1024)
        xf2 = [uf(5120, 1024), uf(6144, 1024)]
        xbo = uf(7168, 1024)
        oscr = uf(8192, 512)
        fscr = uf(8704, 512)
        outst = [uf(9216, 512), uf(9728, 512)]

        wot_r = wot_d.rearrange("(c p) o -> p c o", p=128)
        P.wait("gpsimd", sPE, pair_done_pe[6])
        P.op("gpsimd", lambda e: e.dma_start(
            out=wot_v[:, 0:4096].rearrange("p (c o) -> p c o", c=4),
            in_=wot_r[:, 0:4, :]), inc=sCa, inck=16)
        P.wait("gpsimd", sPE, pair_done_pe[7])
        P.op("gpsimd", lambda e: e.dma_start(
            out=wot_v[:, 4096:8192].rearrange("p (c o) -> p c o", c=4),
            in_=wot_r[:, 4:8, :]), inc=sCa, inck=16)
        wot_ca = P.val(sCa)
        P.wait("sync", sVE, b_ve)
        P.op("sync", lambda e: e.dma_start(out=bo_sb[:, :], in_=bof_d), inc=sLd, inck=16)
        bo_ld = P.val(sLd)

        if stage == "Cw":
            dummy_out()
            finish_emit()
            return nc

        P.wait("tensor", sVE, b_ve)
        t1o_evac = {}
        for blk in range(4):
            r0 = 32 * (blk % 2)
            if blk >= 2:
                P.wait("tensor", sVE, t1o_evac[blk - 2])
            for oc in range(8):
                P.op("tensor", lambda e, blk=blk, oc=oc, r0=r0: e.matmul(
                    ps_pv[r0:r0 + 16, :], aot_s[:, oc * 16:(oc + 1) * 16],
                    numt[:, oc * 2048 + blk * 512: oc * 2048 + (blk + 1) * 512],
                    start=(oc == 0), stop=(oc == 7),
                    tile_position=(0, r0)), inc=(sPE if oc == 7 else None))
            P.wait("vector", sPE, P.val(sPE))
            P.op("vector", lambda e, blk=blk, r0=r0: e.tensor_scalar_mul(
                t1o[0:16, blk * 512:(blk + 1) * 512], ps_pv[r0:r0 + 16, :],
                1.0 / 16.0), inc=sVE)
            t1o_evac[blk] = P.val(sVE)
        t1o_ve = P.val(sVE)

        xf_ld = {}
        xbo_ve = {}

        def emit_xf(tt):
            if tt >= 2:
                P.wait("sync", sVE, xbo_ve[tt - 2])
            else:
                P.wait("sync", sPE, pair_done_pe[7])
            P.op("sync", lambda e, tt=tt: e.dma_start(
                out=xf2[tt % 2][:, :], in_=x_r[:, tt, :]), inc=sLd, inck=16)
            xf_ld[tt] = P.val(sLd)

        emit_xf(0)
        emit_xf(1)

        if stage == "C0":
            dummy_out()
            finish_emit()
            return nc

        P.wait("tensor", sCa, wot_ca)
        P.wait("tensor", sVE, t1o_ve)
        o_evac = {}
        st_done = {}
        ntiles = 1 if stage == "C1" else 16
        for tt in range(ntiles):
            P.wait("vector", sLd, max(xf_ld[tt], bo_ld))
            P.op("vector", lambda e, tt=tt: e.tensor_add(
                xbo[:, :], xf2[tt % 2][:, :], bo_sb[:, :]), inc=sVE)
            xbo_ve[tt] = P.val(sVE)
            for dh in range(2):
                fps = ps_s1[:, dh * 512:(dh + 1) * 512]
                ops_ = ps_s0[:, dh * 512:(dh + 1) * 512]
                P.wait("tensor", sVE, o_evac.get((tt - 1, dh), 0))
                for ic in range(8):
                    P.op("tensor", lambda e, ic=ic, tt=tt, dh=dh, fps=fps: e.matmul(
                        fps, qlt[:, ic * 2048 + tt * 128: ic * 2048 + (tt + 1) * 128],
                        qrh[:, ic * 1024 + dh * 512: ic * 1024 + (dh + 1) * 512],
                        start=(ic == 0), stop=(ic == 7)))
                for oc in range(8):
                    P.op("tensor", lambda e, oc=oc, tt=tt, dh=dh, ops_=ops_: e.matmul(
                        ops_, numt[:, oc * 2048 + tt * 128: oc * 2048 + (tt + 1) * 128],
                        wot_v[:, oc * 1024 + dh * 512: oc * 1024 + (dh + 1) * 512],
                        start=(oc == 0), stop=False))
                P.op("tensor", lambda e, tt=tt, dh=dh, ops_=ops_: e.matmul(
                    ops_, t1o[0:16, tt * 128:(tt + 1) * 128],
                    bot_s[0:16, dh * 512:(dh + 1) * 512],
                    start=False, stop=True), inc=sPE)
                mm_pe = P.val(sPE)
                P.wait("vector", sPE, mm_pe)
                P.wait("vector", sVE, xbo_ve[tt])
                P.op("vector", lambda e, tt=tt, dh=dh, fps=fps: e.scalar_tensor_tensor(
                    fscr[:, :], fps, ls_all[:, tt:tt + 1],
                    rs_full[:, dh * 512:(dh + 1) * 512], AL.mult, AL.mult), inc=sVE)
                P.op("vector", lambda e, dh=dh, ops_=ops_: e.scalar_tensor_tensor(
                    oscr[:, :], ops_, 1.0, xbo[:, dh * 512:(dh + 1) * 512],
                    AL.bypass, AL.add), inc=sVE)
                o_evac[(tt, dh)] = P.val(sVE)
                P.wait("vector", sVE, P.val(sVE))
                ob = outst[dh]
                P.wait("vector", sSt, st_done.get((tt - 1, dh), 0))
                P.op("vector", lambda e, ob=ob: e.tensor_add(ob[:, :], fscr[:, :], oscr[:, :]), inc=sVE)
                P.wait("sync", sVE, P.val(sVE))
                P.op("sync", lambda e, tt=tt, dh=dh, ob=ob: e.dma_start(
                    out=out_r[:, tt, dh * 512:(dh + 1) * 512], in_=ob[:, :]), inc=sSt, inck=16)
                st_done[(tt, dh)] = P.val(sSt)
            if tt + 2 < 16:
                emit_xf(tt + 2)

        if stage == "C1":
            dummy_out()
            finish_emit()
            return nc

        P.wait("sync", sSt, P.val(sSt))

        finish_emit()

    return nc


def _stack_a(aq, ak, av):
    a96 = np.zeros((96, D), dtype=np.float32)
    a96[0:16] = aq
    a96[32:48] = ak
    a96[64:80] = av
    return a96.T


def kernel(**inputs):
    x = np.asarray(inputs["L0_attn_x"])
    wq = np.asarray(inputs["L0_attn_Wq"])
    wk = np.asarray(inputs["L0_attn_Wk"])
    wv = np.asarray(inputs["L0_attn_Wv"])
    wo = np.asarray(inputs["L0_attn_Wo"])
    bo = np.asarray(inputs["L0_attn_bo"])
    aq = np.asarray(inputs["L0_attn_Aq"])
    bq = np.asarray(inputs["L0_attn_Bq"])
    ak = np.asarray(inputs["L0_attn_Ak"])
    bk = np.asarray(inputs["L0_attn_Bk"])
    av = np.asarray(inputs["L0_attn_Av"])
    bv = np.asarray(inputs["L0_attn_Bv"])
    ao = np.asarray(inputs["L0_attn_Ao"])
    bo_l = np.asarray(inputs["L0_attn_Bo"])
    rhs = np.asarray(inputs["L0_ffn_rhs"])

    C = np.ascontiguousarray
    shared = {
        "wqt": C(wq.T), "wkt": C(wk.T), "wvt": C(wv.T), "wot": C(wo.T),
        "aqkvt": C(_stack_a(aq, ak, av)), "aot": C(ao.T),
        "bqt": C(bq.T), "bkt": C(bk.T), "bvt": C(bv.T), "bot": C(bo_l.T),
        "rhst": C(rhs.T),
        "bo_full": C(np.broadcast_to(bo, (128, D)).copy()),
    }

    in_maps = []
    for c in range(8):
        b, half = c // 2, c % 2
        own = x[b, half * 2048:(half + 1) * 2048]
        oth = x[b, (1 - half) * 2048:(2 - half) * 2048]
        m = dict(shared)
        xcat = np.concatenate([own, oth], axis=0)
        m["x"] = C(xcat.reshape(32, 128, D).transpose(1, 0, 2).reshape(NSEQ, D))
        in_maps.append(m)

    nc = build_nc()
    res = run_bass_kernel_spmd(nc, in_maps, core_ids=list(range(8)))

    out = np.empty((4, NSEQ, D), dtype=np.float32)
    for c in range(8):
        b, half = c // 2, c % 2
        out[b, half * 2048:(half + 1) * 2048] = res.results[c]["out"]
    return out


# version 38
# speedup vs baseline: 1.0664x; 1.0036x over previous
import numpy as np
import ml_dtypes
import concourse.bass as bass
import concourse.mybir as mybir
from concourse.bass_utils import run_bass_kernel_spmd

F32 = mybir.dt.float32
BF16 = mybir.dt.bfloat16
BF = ml_dtypes.bfloat16
AL = mybir.AluOpType
ACTF = mybir.ActivationFunctionType
AXX = mybir.AxisListType.X

D = 1024
NSEQ = 4096
TOWN = 2048
GROUPS = 8
OWN_GROUPS = 4
PAIRS = 8
KCH = 32
QB = 4
CR = float(np.float32(1.5 * 2 ** 23))
EPS = 1e-5
MHAT = 25.0


class Plan:
    ENGS = ("sync", "gpsimd", "tensor", "vector", "scalar")

    def __init__(self):
        self.q = {e: [] for e in self.ENGS}
        self.n = {}

    def inc(self, sem, k=1):
        self.n[sem] = self.n.get(sem, 0) + k
        return self.n[sem]

    def val(self, sem):
        return self.n.get(sem, 0)

    def add(self, eng, fn):
        self.q[eng].append(fn)

    def wait(self, eng, sem, v):
        if v > 0:
            self.add(eng, lambda e, s=sem, vv=v: e.wait_ge(s, vv))

    def op(self, eng, build, inc=None, inck=1):
        if inc is not None:
            self.inc(inc, inck)
            self.add(eng, lambda e, b=build, s=inc, k=inck: b(e).then_inc(s, k))
        else:
            self.add(eng, lambda e, b=build: b(e))


def build_nc(stage="full"):
    nc = bass.Bass("TRN2", target_bir_lowering=False, debug=False)

    x_d = nc.dram_tensor("x", [NSEQ, D], F32, kind="ExternalInput").ap()
    wqt_d = nc.dram_tensor("wqt", [D, D], F32, kind="ExternalInput").ap()
    wkt_d = nc.dram_tensor("wkt", [D, D], F32, kind="ExternalInput").ap()
    wvt_d = nc.dram_tensor("wvt", [D, D], F32, kind="ExternalInput").ap()
    wot_d = nc.dram_tensor("wot", [D, D], F32, kind="ExternalInput").ap()
    aqkv_d = nc.dram_tensor("aqkvt", [D, 96], F32, kind="ExternalInput").ap()
    aot_d = nc.dram_tensor("aot", [D, 16], F32, kind="ExternalInput").ap()
    bqt_d = nc.dram_tensor("bqt", [16, D], F32, kind="ExternalInput").ap()
    bkt_d = nc.dram_tensor("bkt", [16, D], F32, kind="ExternalInput").ap()
    bvt_d = nc.dram_tensor("bvt", [16, D], F32, kind="ExternalInput").ap()
    bot_d = nc.dram_tensor("bot", [16, D], F32, kind="ExternalInput").ap()
    rhst_d = nc.dram_tensor("rhst", [D, D], F32, kind="ExternalInput").ap()
    bof_d = nc.dram_tensor("bo_full", [128, D], F32, kind="ExternalInput").ap()
    out_d = nc.dram_tensor("out", [TOWN, D], F32, kind="ExternalOutput").ap()

    kt_h = nc.dram_tensor("kt_h", [D, NSEQ], BF16, kind="Internal").ap()
    v_h = nc.dram_tensor("v_h", [NSEQ, D], BF16, kind="Internal").ap()
    l_h = nc.dram_tensor("l_h", [2, 512], F32, kind="Internal").ap()

    ident_d = nc.inline_tensor(np.eye(128, dtype=np.float32), name="ident")

    x_r = x_d.rearrange("(p a) d -> p a d", a=32)
    v_h_r = v_h.rearrange("(a p) o -> p a o", p=128)
    out_r = out_d.rearrange("(a p) d -> p a d", p=128)

    UW = 27648

    from contextlib import ExitStack
    ctx = ExitStack()
    sb = lambda *a: ctx.enter_context(nc.sbuf_tensor(*a))
    pst = lambda *a: ctx.enter_context(nc.psum_tensor(*a))
    sem = lambda n: ctx.enter_context(nc.semaphore(n))
    with ctx:
        qt = sb("qt", [128, 8 * 2048], BF16)
        qlt = sb("qlt", [128, 8 * 2048], BF16)
        qrh = sb("qrh", [128, 8 * 1024], BF16)
        rs_full = sb("rs_full", [128, 1024], F32)
        ls_all = sb("ls_all", [128, 16], F32)
        stats = sb("stats", [128, 160], F32)
        ones_b = sb("ones_b", [128, 64], BF16)
        ones_f = sb("ones_f", [128, 128], F32)
        ident_f = sb("ident_f", [128, 128], F32)
        biasx = sb("biasx", [128, 4], F32)
        aqkv_s = sb("aqkv_s", [128, 768], BF16)
        aot_s = sb("aot_s", [128, 128], BF16)
        bqt_s = sb("bqt_s", [128, 1024], BF16)
        bkt_s = sb("bkt_s", [128, 1024], BF16)
        bvt_s = sb("bvt_s", [128, 1024], BF16)
        bot_s = sb("bot_s", [128, 1024], BF16)
        t1s = sb("t1s", [128, 512], BF16)
        U = sb("U", [128, UW], F32)
        ps_s0 = pst("ps_s0", [128, 1024], F32)
        ps_s1 = pst("ps_s1", [128, 1024], F32)
        ps_s2 = pst("ps_s2", [128, 1024], F32)
        ps_pv = pst("ps_pv", [128, 512], F32)
        ps_l = pst("ps_l", [128, 512], F32)
        sLd = sem("sLd")
        sCa = sem("sCa")
        sSt = sem("sSt")
        sPE = sem("sPE")
        sVE = sem("sVE")
        sAC = sem("sAC")
        block = ctx.enter_context(nc.Block())

        P = Plan()

        def uf(off, n):
            ap = U[:, off:off + n]
            return ap

        def ui(off, n):
            ap = U[:, off:off + n].bitcast(mybir.dt.int32)
            return ap

        def ub(off, nbf):
            ap = U[:, off:off + nbf // 2].bitcast(BF16)
            assert tuple(ap.shape) == (128, nbf), ap.shape
            return ap

        P.op("sync", lambda e: e.dma_start(out=ident_f[:, :], in_=ident_d.ap()), inc=sLd, inck=16)
        P.op("vector", lambda e: e.memset(biasx[:, :], -MHAT))
        P.op("vector", lambda e: e.memset(ones_b[:, :], 1.0))
        P.op("vector", lambda e: e.memset(ones_f[:, :], 1.0), inc=sVE)
        init_ve = P.val(sVE)
        P.wait("scalar", sVE, init_ve)
        P.op("scalar", lambda e: e.activation(
            stats[:, 156:157], ones_f[:, 0:1], ACTF.Sqrt), inc=sAC)
        rhst_v = uf(4096, 8192)
        rhst_state = {}

        wq_v = ub(0, 8192)
        wk_v = ub(4096, 8192)
        wv_v = ub(8192, 8192)
        xg = [uf(12288, 4096), uf(16384, 4096)]
        xnT2 = [ub(20480, 4096), ub(22528, 4096)]
        qs2b = [uf(24576, 1024), uf(25600, 1024)]
        kvst = [ub(26624, 512), ub(26880, 512), ub(27136, 512), ub(27392, 512)]

        wload = {}

        def emit_lora_loads():
            P.op("gpsimd", lambda e: e.dma_start(
                out=aqkv_s[:, :].rearrange("p (c r) -> p c r", c=8),
                in_=aqkv_d.rearrange("(c p) r -> p c r", p=128)), inc=sCa, inck=16)
            P.op("gpsimd", lambda e: e.dma_start(
                out=aot_s[:, :].rearrange("p (c r) -> p c r", c=8),
                in_=aot_d.rearrange("(c p) r -> p c r", p=128)), inc=sCa, inck=16)
            for dst, src_, r0 in ((bqt_s, bqt_d, 0), (bkt_s, bkt_d, 32),
                                  (bvt_s, bvt_d, 64), (bot_s, bot_d, 0)):
                P.op("gpsimd", lambda e, d=dst, s=src_, r0=r0: e.dma_start(
                    out=d[r0:r0 + 16, :], in_=s), inc=sCa, inck=16)
            return P.val(sCa)

        def emit_heavy_wloads():
            for nm, dst, src_, tpj in (("k", wk_v, wkt_d, 3), ("v", wv_v, wvt_d, 7)):
                P.wait("gpsimd", sPE, a1_tp[tpj])
                P.op("gpsimd", lambda e, d=dst, s=src_: e.dma_start(
                    out=d.rearrange("p (c o) -> p c o", c=8),
                    in_=s.rearrange("(c p) o -> p c o", p=128)), inc=sCa, inck=16)
                wload[nm] = P.val(sCa)

        xload_ld = {}
        xload_ca = {}
        xg_free = {}

        def emit_xload(g):
            if g >= 2:
                pe_v, ve_v = xg_free[g - 2]
                P.wait("sync", sPE, pe_v)
                P.wait("sync", sVE, ve_v)
            if g == 0:
                P.op("sync", lambda e, g=g: e.dma_start(
                    out=xg[g % 2][:, 0:2048].rearrange("p (a d) -> p a d", a=2),
                    in_=x_r[:, 4 * g:4 * g + 2, :]), inc=sLd, inck=16)
                P.op("gpsimd", lambda e, g=g: e.dma_start(
                    out=xg[g % 2][:, 2048:4096].rearrange("p (a d) -> p a d", a=2),
                    in_=x_r[:, 4 * g + 2:4 * g + 4, :]), inc=sCa, inck=16)
                xload_ca[g] = P.val(sCa)
            else:
                P.op("sync", lambda e, g=g: e.dma_start(
                    out=xg[g % 2].rearrange("p (a d) -> p a d", a=4),
                    in_=x_r[:, 4 * g:4 * g + 4, :]), inc=sLd, inck=16)
            xload_ld[g] = P.val(sLd)

        emit_xload(0)
        rhst_r = rhst_d.rearrange("(c p) i -> p c i", p=128)
        P.op("gpsimd", lambda e: e.dma_start(
            out=rhst_v[:, 0:4096].rearrange("p (c i) -> p c i", c=4),
            in_=rhst_r[:, 0:4, :]), inc=sCa, inck=16)
        rhst_state["ca"] = P.val(sCa)
        P.op("sync", lambda e: e.dma_start(
            out=rhst_v[:, 4096:8192].rearrange("p (c i) -> p c i", c=4),
            in_=rhst_r[:, 4:8, :]), inc=sLd, inck=16)
        rhst_state["ld"] = P.val(sLd)
        emit_xload(1)
        lora_ca = emit_lora_loads()
        P.op("gpsimd", lambda e: e.dma_start(
            out=wq_v.rearrange("p (c o) -> p c o", c=8),
            in_=wqt_d.rearrange("(c p) o -> p c o", p=128)), inc=sCa, inck=16)
        wload["q"] = P.val(sCa)

        ps_slots = [ps_s0[:, 0:512], ps_s0[:, 512:1024], ps_s1[:, 0:512], ps_s1[:, 512:1024]]
        slot_evac = [0, 0, 0, 0]
        slot_i = [0]
        stage_hist = []
        k_st = {}
        v_st = {}
        s2_free_ac = [0]

        def proj_tile(mms, evac_fn, fold_gate=None):
            si = slot_i[0] % 4
            slot_i[0] += 1
            ps = ps_slots[si]
            P.wait("tensor", sAC, slot_evac[si])
            n = len(mms)
            for i, mm in enumerate(mms):
                if fold_gate is not None and i == n - 1:
                    P.wait("tensor", fold_gate[0], fold_gate[1])
                lh, rh = mm[0], mm[1]
                tp = mm[2] if len(mm) > 2 else None
                P.op("tensor", lambda e, lh=lh, rh=rh, i=i, ps=ps, n=n, tp=tp:
                     e.matmul(ps, lh, rh, start=(i == 0), stop=(i == n - 1),
                              tile_position=tp),
                     inc=(sPE if i == n - 1 else None))
            P.wait("scalar", sPE, P.val(sPE))
            P.op("scalar", lambda e, ps=ps: evac_fn(e, ps), inc=sAC)
            slot_evac[si] = P.val(sAC)

        last_t1 = [0]

        def lora_t1(xnT, ready_ac):
            P.wait("tensor", sCa, lora_ca)
            P.wait("tensor", sAC, ready_ac)
            P.wait("tensor", sAC, last_t1[0])
            for ic in range(8):
                P.op("tensor", lambda e, ic=ic, xnT=xnT: e.matmul(
                    ps_pv[0:96, :], aqkv_s[:, ic * 96:(ic + 1) * 96],
                    xnT[:, ic * 512:(ic + 1) * 512],
                    start=(ic == 0), stop=(ic == 7)), inc=(sPE if ic == 7 else None))
            P.wait("scalar", sPE, P.val(sPE))
            P.op("scalar", lambda e: e.mul(t1s[0:96, :], ps_pv[0:96, :], 1.0 / 16.0), inc=sAC)
            last_t1[0] = P.val(sAC)
            return P.val(sAC)

        def kv_stage_store():
            idx = len(stage_hist)
            st = kvst[idx % 4]
            wait_st = stage_hist[idx - 4] if idx >= 4 else 0
            P.wait("scalar", sSt, wait_st)

            def evac(e, ps, st=st):
                return e.copy(st[:, :], ps)
            return st, evac

        s2_half = [0, 0]
        s2_par = [0]

        def transpose_batch(src_ap, dst_fn, in_ve):
            P.wait("tensor", sVE, in_ve)
            tp = 0
            for h in range(2):
                par = s2_par[0]
                s2_par[0] ^= 1
                P.wait("tensor", sAC, s2_half[par])
                for ic in range(4):
                    icg = h * 4 + ic
                    P.op("tensor", lambda e, icg=icg, ic=ic, s=src_ap, par=par: e.transpose(
                        ps_s2[:, par * 512 + ic * 128: par * 512 + (ic + 1) * 128],
                        s[:, icg * 128:(icg + 1) * 128],
                        ident_f[:, :]), inc=(sPE if ic == 3 else None))
                tp = P.val(sPE)
                P.wait("scalar", sPE, tp)
                P.op("scalar", lambda e, h=h, par=par: dst_fn(
                    e, h, ps_s2[:, par * 512:(par + 1) * 512]), inc=sAC)
                s2_half[par] = P.val(sAC)
            return tp

        a1_state = {}
        a1_tp = {}

        def emit_a1():
            for hh in range(2):
                if hh == 0:
                    P.wait("vector", sCa, rhst_state["ca"])
                else:
                    P.wait("vector", sLd, rhst_state["ld"])
                c0 = hh * 4
                P.op("vector", lambda e, c0=c0, hh=hh: e.reduce_max(
                    stats[:, c0:c0 + 4],
                    rhst_v[:, hh * 4096:(hh + 1) * 4096].rearrange("p (c i) -> p c i", c=4),
                    axis=AXX, apply_absolute_value=True), inc=sVE)
                P.wait("vector", sVE, P.val(sVE))
                P.op("vector", lambda e, c0=c0: e.tensor_scalar_max(
                    stats[:, 8 + c0:8 + c0 + 4], stats[:, c0:c0 + 4], 1e-12), inc=sVE)
                P.wait("vector", sVE, P.val(sVE))
                P.op("vector", lambda e, c0=c0: e.reciprocal(
                    stats[:, 16 + c0:16 + c0 + 4], stats[:, 8 + c0:8 + c0 + 4]), inc=sVE)
                P.wait("vector", sVE, P.val(sVE))
                P.op("vector", lambda e, c0=c0: e.tensor_scalar_mul(
                    stats[:, 24 + c0:24 + c0 + 4], stats[:, 16 + c0:16 + c0 + 4], 127.0))
                P.op("vector", lambda e, c0=c0: e.tensor_scalar_mul(
                    stats[:, 32 + c0:32 + c0 + 4], stats[:, 8 + c0:8 + c0 + 4],
                    1.0 / 127.0), inc=sVE)
                a1_state["sc_ve%d" % hh] = P.val(sVE)
                P.wait("vector", sVE, P.val(sVE))

            def emit_j(j):
                sl = rhst_v[:, j * 1024:(j + 1) * 1024]
                sli = sl.bitcast(mybir.dt.int32)
                P.op("vector", lambda e, j=j, sli=sli, sl=sl: e.tensor_scalar_mul(
                    sli, sl, stats[:, 24 + j:25 + j]), inc=sVE)
                P.wait("vector", sVE, P.val(sVE))
                P.op("vector", lambda e, sl=sl, sli=sli: e.tensor_copy(sl, sli), inc=sVE)
                qv = P.val(sVE)

                def evac_rh(e, h, psh, j=j):
                    return e.copy(
                        qrh.rearrange("p (c o) -> p c o", c=8)[:, 4 * h:4 * h + 4,
                                                               j * 128:(j + 1) * 128],
                        psh.rearrange("p (c t) -> p c t", c=4))
                a1_tp[j] = transpose_batch(sl, evac_rh, qv)

            rs_row = uf(26624, 1024)
            rst8 = uf(26752, 128)
            for j in range(2):
                emit_j(j)
            par = s2_par[0]
            s2_par[0] ^= 1
            P.wait("tensor", sAC, s2_half[par])
            P.wait("tensor", sVE, a1_state["sc_ve1"])
            P.op("tensor", lambda e, par=par: e.transpose(
                ps_s2[0:8, par * 512:par * 512 + 128], stats[:, 32:40],
                ident_f[:, :]), inc=sPE)
            rs_tp = P.val(sPE)
            P.wait("scalar", sPE, rs_tp)
            P.op("scalar", lambda e, par=par: e.copy(
                rst8[0:8, 0:128], ps_s2[0:8, par * 512:par * 512 + 128]), inc=sAC)
            s2_half[par] = P.val(sAC)
            P.wait("sync", sAC, P.val(sAC))
            for j in range(8):
                P.op("sync", lambda e, j=j: e.dma_start(
                    out=rs_row[0:1, j * 128:(j + 1) * 128], in_=rst8[j:j + 1, 0:128]),
                    inc=sSt, inck=16)
            rs_st = P.val(sSt)
            for j in range(2, 4):
                emit_j(j)
            P.wait("tensor", sSt, rs_st)
            P.wait("tensor", sAC, s2_half[0])
            P.wait("tensor", sAC, s2_half[1])
            P.op("tensor", lambda e: e.matmul(ps_s2[:, 0:512], ones_f[0:1, 0:128],
                                              rs_row[0:1, 0:512], start=True, stop=True))
            P.op("tensor", lambda e: e.matmul(ps_s2[:, 512:1024], ones_f[0:1, 0:128],
                                              rs_row[0:1, 512:1024], start=True, stop=True), inc=sPE)
            P.wait("scalar", sPE, P.val(sPE))
            P.op("scalar", lambda e: e.copy(rs_full[:, :], ps_s2[:, :]), inc=sAC)
            s2_half[0] = P.val(sAC)
            s2_half[1] = P.val(sAC)
            for j in range(4, 8):
                emit_j(j)

        qtrans_pe = {}
        qs2b_guard = [0]

        def sumsq(tt, xgv, b0):
            for hv in range(2):
                P.op("vector", lambda e, tt=tt, hv=hv, xgv=xgv, b0=b0: e.scalar_tensor_tensor(
                    ps_l[:, :], xgv[:, tt * 1024 + hv * 512:tt * 1024 + (hv + 1) * 512],
                    1.0, xgv[:, tt * 1024 + hv * 512:tt * 1024 + (hv + 1) * 512],
                    AL.bypass, AL.mult,
                    accum_out=stats[:, b0 + 48 + 2 * tt + hv:b0 + 49 + 2 * tt + hv]))

        def fe_stats(g):
            b0 = 40 + 56 * (g % 2)
            xgv = xg[g % 2]
            if g in xload_ca:
                P.wait("vector", sLd, xload_ld[g])
                for tt in range(2):
                    sumsq(tt, xgv, b0)
                P.op("vector", lambda e, xgv=xgv, b0=b0: e.reduce_sum(
                    stats[:, b0 + 4:b0 + 6],
                    xgv[:, 0:2048].rearrange("p (a d) -> p a d", a=2), axis=AXX))
                P.wait("vector", sCa, xload_ca[g])
                for tt in range(2, 4):
                    sumsq(tt, xgv, b0)
                P.op("vector", lambda e, xgv=xgv, b0=b0: e.reduce_sum(
                    stats[:, b0 + 6:b0 + 8],
                    xgv[:, 2048:4096].rearrange("p (a d) -> p a d", a=2), axis=AXX), inc=sVE)
            else:
                P.wait("vector", sLd, xload_ld[g])
                for tt in range(4):
                    sumsq(tt, xgv, b0)
                P.op("vector", lambda e, xgv=xgv, b0=b0: e.reduce_sum(
                    stats[:, b0 + 4:b0 + 8], xgv.rearrange("p (a d) -> p a d", a=4), axis=AXX), inc=sVE)
            P.wait("vector", sVE, P.val(sVE))
            P.op("vector", lambda e, b0=b0: e.tensor_add(
                stats[:, b0:b0 + 4], stats[:, b0 + 48:b0 + 56:2],
                stats[:, b0 + 49:b0 + 56:2]), inc=sVE)
            P.wait("vector", sVE, P.val(sVE))
            P.op("vector", lambda e, b0=b0: e.tensor_scalar_mul(
                stats[:, b0 + 8:b0 + 12], stats[:, b0 + 4:b0 + 8], 1.0 / D))
            P.op("vector", lambda e, b0=b0: e.tensor_scalar(
                stats[:, b0 + 12:b0 + 16], stats[:, b0:b0 + 4], 1.0 / D, EPS,
                AL.mult, AL.add), inc=sVE)
            P.wait("vector", sVE, P.val(sVE))
            P.op("vector", lambda e, b0=b0: e.tensor_mul(
                stats[:, b0 + 16:b0 + 20], stats[:, b0 + 8:b0 + 12], stats[:, b0 + 8:b0 + 12]), inc=sVE)
            P.wait("vector", sVE, P.val(sVE))
            P.op("vector", lambda e, b0=b0: e.tensor_sub(
                stats[:, b0 + 20:b0 + 24], stats[:, b0 + 12:b0 + 16], stats[:, b0 + 16:b0 + 20]), inc=sVE)
            P.wait("vector", sVE, P.val(sVE))
            P.op("vector", lambda e, b0=b0: e.reciprocal(
                stats[:, b0 + 24:b0 + 28], stats[:, b0 + 20:b0 + 24]), inc=sVE)
            P.wait("scalar", sVE, P.val(sVE))
            P.op("scalar", lambda e, b0=b0: e.activation(
                stats[:, b0 + 28:b0 + 32], stats[:, b0 + 24:b0 + 28], ACTF.Sqrt), inc=sAC)
            P.wait("vector", sAC, P.val(sAC))
            P.op("vector", lambda e, b0=b0: e.tensor_mul(
                stats[:, b0 + 32:b0 + 36], stats[:, b0 + 28:b0 + 32], stats[:, b0 + 28:b0 + 32]), inc=sVE)
            P.wait("vector", sVE, P.val(sVE))
            P.op("vector", lambda e, b0=b0: e.tensor_mul(
                stats[:, b0 + 36:b0 + 40], stats[:, b0 + 32:b0 + 36], stats[:, b0 + 20:b0 + 24]), inc=sVE)
            P.wait("vector", sVE, P.val(sVE))
            P.op("vector", lambda e, b0=b0: e.tensor_scalar(
                stats[:, b0 + 40:b0 + 44], stats[:, b0 + 36:b0 + 40], -0.5, 1.5,
                AL.mult, AL.add), inc=sVE)
            P.wait("vector", sVE, P.val(sVE))
            P.op("vector", lambda e, b0=b0: e.tensor_mul(
                stats[:, b0 + 44:b0 + 48], stats[:, b0 + 28:b0 + 32], stats[:, b0 + 40:b0 + 44]), inc=sVE)
            P.wait("vector", sVE, P.val(sVE))

        fe_lasttp = {}

        def frontend_xnT(g):
            b0 = 40 + 56 * (g % 2)
            xnT = xnT2[g % 2]
            xgv = xg[g % 2]
            xn_ve = {}
            for tt in range(4):
                P.op("vector", lambda e, tt=tt, xgv=xgv, b0=b0: e.tensor_scalar(
                    xgv[:, tt * 1024:(tt + 1) * 1024], xgv[:, tt * 1024:(tt + 1) * 1024],
                    stats[:, b0 + 8 + tt:b0 + 9 + tt], stats[:, b0 + 44 + tt:b0 + 45 + tt],
                    AL.subtract, AL.mult), inc=sVE)
                xn_ve[tt] = P.val(sVE)

            last_tp = 0
            for tt in range(4):
                def evac_xn(e, h, psh, tt=tt, xnT=xnT):
                    return e.copy(
                        xnT.rearrange("p (c t) -> p c t", c=8)[:, 4 * h:4 * h + 4,
                                                               tt * 128:(tt + 1) * 128],
                        psh.rearrange("p (c t) -> p c t", c=4))
                last_tp = transpose_batch(xgv[:, tt * 1024:(tt + 1) * 1024], evac_xn, xn_ve[tt])
            xnT_ac_of[g] = P.val(sAC)
            fe_lasttp[g] = last_tp

        def frontend_quant(g):
            own = g < OWN_GROUPS
            xgv = xg[g % 2]
            quant_ve = 0
            if own:
                for tt in range(4):
                    gt = g * 4 + tt
                    qsb = qs2b[tt % 2]
                    if tt >= 2:
                        P.wait("vector", sPE, qtrans_pe[(g, tt - 2)])
                    P.op("vector", lambda e, tt=tt, xgv=xgv: e.reduce_max(
                        stats[:, 152:153], xgv[:, tt * 1024:(tt + 1) * 1024], axis=AXX,
                        apply_absolute_value=True), inc=sVE)
                    P.wait("vector", sVE, P.val(sVE))
                    P.op("vector", lambda e: e.tensor_scalar_max(
                        stats[:, 153:154], stats[:, 152:153], 1e-12), inc=sVE)
                    P.wait("vector", sVE, P.val(sVE))
                    P.op("vector", lambda e: e.reciprocal(stats[:, 154:155], stats[:, 153:154]), inc=sVE)
                    P.wait("vector", sVE, P.val(sVE))
                    P.op("vector", lambda e: e.tensor_scalar_mul(stats[:, 155:156], stats[:, 154:155], 127.0))
                    P.op("vector", lambda e, gt=gt: e.tensor_scalar_mul(
                        ls_all[:, gt:gt + 1], stats[:, 153:154], 1.0 / 127.0), inc=sVE)
                    P.wait("vector", sVE, P.val(sVE))
                    qsbi = qsb.bitcast(mybir.dt.int32)
                    P.op("vector", lambda e, tt=tt, xgv=xgv, qsbi=qsbi: e.tensor_scalar_mul(
                        qsbi, xgv[:, tt * 1024:(tt + 1) * 1024],
                        stats[:, 155:156]), inc=sVE)
                    P.wait("vector", sVE, P.val(sVE))
                    P.op("vector", lambda e, qsb=qsb, qsbi=qsbi: e.tensor_copy(qsb[:, :], qsbi), inc=sVE)
                    qv = P.val(sVE)

                    def evac_q(e, h, psh, gt=gt):
                        return e.copy(
                            qlt.rearrange("p (c t) -> p c t", c=8)[:, 4 * h:4 * h + 4,
                                                                   gt * 128:(gt + 1) * 128],
                            psh.rearrange("p (c t) -> p c t", c=4))
                    qtrans_pe[(g, tt)] = transpose_batch(qsb, evac_q, qv)
                qs2b_guard[0] = qtrans_pe[(g, 3)]
                quant_ve = P.val(sVE)
            xg_free[g] = (qtrans_pe[(g, 3)] if own else fe_lasttp[g], quant_ve)
            if g + 2 < GROUPS:
                emit_xload(g + 2)

        def frontend(g):
            frontend_xnT(g)
            frontend_quant(g)


        def backend(g, t1_ac):
            own = g < OWN_GROUPS
            xnT = xnT2[g % 2]
            P.wait("tensor", sCa, wload["k"])
            for oc in range(8):
                mms = [(wk_v[:, ic * 1024 + oc * 128: ic * 1024 + (oc + 1) * 128],
                        xnT[:, ic * 512:(ic + 1) * 512]) for ic in range(8)]
                mms.append((bkt_s[32:48, oc * 128:(oc + 1) * 128], t1s[32:48, :], (32, 0)))
                st, evac = kv_stage_store()
                proj_tile(mms, evac, fold_gate=(sAC, t1_ac) if oc == 0 else None)
                P.wait("sync", sAC, P.val(sAC))
                P.op("sync", lambda e, oc=oc, g=g, st=st: e.dma_start(
                    out=kt_h[oc * 128:(oc + 1) * 128, g * 512:(g + 1) * 512],
                    in_=st[:, :]), inc=sSt, inck=16)
                stage_hist.append(P.val(sSt))
                k_st[(g, oc)] = P.val(sSt)

            P.wait("tensor", sCa, wload["v"])
            for oh in range(2):
                for tt in range(4):
                    mms = [(xnT[:, ic * 512 + tt * 128: ic * 512 + (tt + 1) * 128],
                            wv_v[:, ic * 1024 + oh * 512: ic * 1024 + (oh + 1) * 512])
                           for ic in range(8)]
                    mms.append((t1s[64:80, tt * 128:(tt + 1) * 128],
                                bvt_s[64:80, oh * 512:(oh + 1) * 512], (64, 0)))
                    st, evac = kv_stage_store()
                    proj_tile(mms, evac)
                    P.wait("sync", sAC, P.val(sAC))
                    P.op("sync", lambda e, g=g, tt=tt, oh=oh, st=st: e.dma_start(
                        out=v_h_r[:, g * 4 + tt, oh * 512:(oh + 1) * 512],
                        in_=st[:, :]), inc=sSt, inck=16)
                    stage_hist.append(P.val(sSt))
                    v_st[(g, tt, oh)] = P.val(sSt)

            if own:
                P.wait("tensor", sCa, wload["q"])
                for oc in range(8):
                    mms = [(wq_v[:, ic * 1024 + oc * 128: ic * 1024 + (oc + 1) * 128],
                            xnT[:, ic * 512:(ic + 1) * 512]) for ic in range(8)]
                    mms.append((bqt_s[0:16, oc * 128:(oc + 1) * 128], t1s[0:16, :]))

                    def evac_q2(e, ps, oc=oc, g=g):
                        return e.copy(
                            qt[:, oc * 2048 + g * 512: oc * 2048 + (g + 1) * 512], ps)
                    proj_tile(mms, evac_q2)


        xnT_ac_of = {}
        fe_stats(0)
        frontend_xnT(0)
        emit_a1()
        emit_heavy_wloads()
        frontend_quant(0)
        fe_stats(1)
        for g in range(GROUPS):
            t1_ac = lora_t1(xnT2[g % 2], xnT_ac_of[g])
            if g + 1 < GROUPS:
                frontend(g + 1)
            if g + 2 < GROUPS:
                fe_stats(g + 2)
            backend(g, t1_ac)

        a2_st = P.val(sSt)
        a2_ve = P.val(sVE)
        a2_ac = P.val(sAC)

        def dummy_out():
            P.wait("sync", sVE, P.val(sVE))
            P.wait("sync", sPE, P.val(sPE))
            P.wait("sync", sAC, P.val(sAC))
            for tt in range(16):
                P.op("sync", lambda e, tt=tt: e.dma_start(
                    out=out_r[:, tt, :], in_=bo_sb[:, :]), inc=sSt, inck=16)
            P.wait("sync", sSt, P.val(sSt))

        def finish_emit():
            @block.sync
            def _(e):
                for fn in P.q["sync"]:
                    fn(e)

            @block.gpsimd
            def _(e):
                for fn in P.q["gpsimd"]:
                    fn(e)

            @block.tensor
            def _(e):
                for fn in P.q["tensor"]:
                    fn(e)

            @block.vector
            def _(e):
                for fn in P.q["vector"]:
                    fn(e)

            @block.scalar
            def _(e):
                for fn in P.q["scalar"]:
                    fn(e)

        if stage == "A":
            dummy_out()
            finish_emit()
            return nc

        ktb = [ub(0, 4096), ub(2048, 4096)]
        vsb = [ub(4096, 4096), ub(6144, 4096)]
        pbuf = [ub(8192, 1024), ub(8704, 1024), ub(9216, 1024)]
        l_sp = uf(10240, 512)
        rl_full = uf(10752, 512)
        numt = ub(11264, 16384)

        kv_load_ld = {}
        kt_ld = {}

        def emit_kv_load(p, after_pe):
            P.wait("gpsimd", sPE, after_pe)
            P.wait("gpsimd", sSt, k_st[(GROUPS - 1, p)])
            P.op("gpsimd", lambda e, p=p: e.dma_start(
                out=ktb[p % 2][:, :], in_=kt_h[p * 128:(p + 1) * 128, :]), inc=sCa, inck=16)
            kt_ld[p] = P.val(sCa)
            P.wait("gpsimd", sSt, v_st[(GROUPS - 1, 3, p // 4)])
            P.op("gpsimd", lambda e, p=p: e.dma_start(
                out=vsb[p % 2][:, :].rearrange("p (a o) -> p a o", a=32),
                in_=v_h_r[:, :, p * 128:(p + 1) * 128]), inc=sCa, inck=16)
            kv_load_ld[p] = P.val(sCa)

        emit_kv_load(0, 0)
        emit_kv_load(1, 0)

        P.wait("tensor", sVE, a2_ve)
        P.wait("tensor", sAC, a2_ac)
        P.wait("scalar", sVE, init_ve)

        pair_done_pe = {}
        sring = [ps_s0, ps_s1]
        PV_banks = [ps_pv, ps_s2[:, 0:512]]
        L_banks = [ps_l, ps_s2[:, 512:1024]]
        exp_of = {}
        pcons = {}
        scores_pe = {}
        dance_done = [0, 0]
        prev_repl_pe = [0]
        pending = [None]

        def flush_dance():
            Lb, PVb, pi, p_, qb_, l_ve = pending[0]
            pending[0] = None
            P.wait("gpsimd", sVE, l_ve)
            P.op("gpsimd", lambda e: e.dma_start(
                out=l_h[0:1, :], in_=l_sp[0:1, :]), inc=sCa, inck=16)
            P.op("gpsimd", lambda e: e.dma_start(
                out=l_h[1:2, :], in_=l_sp[32:33, :]), inc=sCa, inck=16)
            st_ca = P.val(sCa)
            P.wait("gpsimd", sCa, st_ca)
            P.op("gpsimd", lambda e: e.dma_start(
                out=rl_full[0:64, :],
                in_=l_h[0:1, :].broadcast_to([64, 512])), inc=sCa, inck=16)
            P.op("gpsimd", lambda e: e.dma_start(
                out=rl_full[64:128, :],
                in_=l_h[1:2, :].broadcast_to([64, 512])), inc=sCa, inck=16)
            prev_repl_pe[0] = st_ca
            P.wait("vector", sCa, P.val(sCa))
            P.op("vector", lambda e: e.reciprocal(rl_full[:, :], rl_full[:, :]), inc=sVE)
            P.wait("vector", sVE, P.val(sVE))
            P.op("vector", lambda e, p_=p_, qb_=qb_, PVb=PVb: e.tensor_mul(
                numt[:, p_ * 2048 + qb_ * 512: p_ * 2048 + (qb_ + 1) * 512],
                PVb[:, :], rl_full[:, :]), inc=sVE)
            dance_done[pi] = P.val(sVE)

        NCH = QB * KCH
        TOT = PAIRS * NCH

        def emit_scores(c):
            p_, j_ = divmod(c, NCH)
            qb_, kc_ = divmod(j_, KCH)
            if j_ == 0:
                P.wait("tensor", sCa, kt_ld[p_])
            kb = ktb[p_ % 2]
            S = sring[c % 2]
            if c >= 2 and (j_ < 2):
                P.wait("tensor", sAC, exp_of[c - 2])
            qa = qt[0:64, p_ * 2048 + qb_ * 512: p_ * 2048 + (qb_ + 1) * 512]
            qbb = qt[64:128, p_ * 2048 + qb_ * 512: p_ * 2048 + (qb_ + 1) * 512]
            P.op("tensor", lambda e, kc=kc_, qa=qa, S=S, kb=kb: e.matmul(
                S[:, 0:512], kb[0:64, kc * 128:(kc + 1) * 128], qa,
                start=True, stop=True, tile_position=(0, 0)))
            P.op("tensor", lambda e, kc=kc_, qbb=qbb, S=S, kb=kb: e.matmul(
                S[:, 512:1024], kb[64:128, kc * 128:(kc + 1) * 128], qbb,
                start=True, stop=True, tile_position=(64, 0)), inc=sPE)
            scores_pe[c] = P.val(sPE)

        emit_scores(0)
        emit_scores(1)
        for c in range(TOT):
            p, j = divmod(c, NCH)
            qb, kc = divmod(j, KCH)
            pi = qb % 2
            S = sring[c % 2]
            PVb = PV_banks[pi]
            Lb = L_banks[pi]
            vb = vsb[p % 2]
            P.wait("scalar", sPE, scores_pe[c])
            pb = pbuf[c % 3]
            P.op("scalar", lambda e, S=S, pb=pb: e.activation(
                pb[:, :], S[:, :], ACTF.Exp, bias=biasx[:, 0:1], scale=0.125), inc=sAC)
            exp_of[c] = P.val(sAC)
            P.wait("tensor", sAC, exp_of[c])
            if c + 2 < TOT:
                emit_scores(c + 2)
            st = kc == 0
            sp = kc == KCH - 1
            if j == 0:
                P.wait("tensor", sCa, kv_load_ld[p])
            if st:
                P.wait("tensor", sVE, dance_done[pi])
            P.op("tensor", lambda e, vb=vb, kc=kc, pb=pb, st=st, sp=sp, PVb=PVb: e.matmul(
                PVb[0:64, :], vb[:, kc * 128:kc * 128 + 64], pb[:, 0:512],
                start=st, stop=sp, tile_position=(0, 0)))
            P.op("tensor", lambda e, vb=vb, kc=kc, pb=pb, st=st, sp=sp, PVb=PVb: e.matmul(
                PVb[64:128, :], vb[:, kc * 128 + 64:kc * 128 + 128], pb[:, 512:1024],
                start=st, stop=sp, tile_position=(0, 64)))
            P.op("tensor", lambda e, pb=pb, st=st, sp=sp, Lb=Lb: e.matmul(
                Lb[0:1, :], ones_b[:, 0:1], pb[:, 0:512],
                start=st, stop=sp, tile_position=(0, 0)))
            P.op("tensor", lambda e, pb=pb, st=st, sp=sp, Lb=Lb: e.matmul(
                Lb[32:33, :], ones_b[:, 0:1], pb[:, 512:1024],
                start=st, stop=sp, tile_position=(0, 32)), inc=sPE)
            pcons[c] = P.val(sPE)
            if kc == 4 and pending[0] is not None:
                flush_dance()
            if kc == KCH - 1:
                P.wait("vector", sPE, pcons[c])
                P.wait("vector", sCa, prev_repl_pe[0])
                P.op("vector", lambda e, Lb=Lb: e.tensor_copy(l_sp[0:1, :], Lb[0:1, :]))
                P.op("vector", lambda e, Lb=Lb: e.tensor_copy(l_sp[32:33, :], Lb[32:33, :]), inc=sVE)
                pending[0] = (Lb, PVb, pi, p, qb, P.val(sVE))
            if j == NCH - 1:
                pair_done_pe[p] = P.val(sPE)
                if p == PAIRS - 1 and pending[0] is not None:
                    flush_dance()
                if p + 2 < PAIRS:
                    emit_kv_load(p + 2, pair_done_pe[p])
        b_ve = P.val(sVE)

        if stage == "B":
            dummy_out()
            finish_emit()
            return nc

        wot_v = ub(0, 8192)
        t1o = ub(4096, 2048)
        bo_sb = uf(19456, 1024)
        xf2 = [uf(5120, 1024), uf(6144, 1024)]
        xbo = uf(7168, 1024)
        oscr = uf(8192, 512)
        fscr = uf(8704, 512)
        outst = [uf(9216, 512), uf(9728, 512)]

        wot_r = wot_d.rearrange("(c p) o -> p c o", p=128)
        P.wait("gpsimd", sPE, pair_done_pe[6])
        P.op("gpsimd", lambda e: e.dma_start(
            out=wot_v[:, 0:4096].rearrange("p (c o) -> p c o", c=4),
            in_=wot_r[:, 0:4, :]), inc=sCa, inck=16)
        P.wait("gpsimd", sPE, pair_done_pe[7])
        P.op("gpsimd", lambda e: e.dma_start(
            out=wot_v[:, 4096:8192].rearrange("p (c o) -> p c o", c=4),
            in_=wot_r[:, 4:8, :]), inc=sCa, inck=16)
        wot_ca = P.val(sCa)
        P.wait("sync", sVE, b_ve)
        P.op("sync", lambda e: e.dma_start(out=bo_sb[:, :], in_=bof_d), inc=sLd, inck=16)
        bo_ld = P.val(sLd)

        if stage == "Cw":
            dummy_out()
            finish_emit()
            return nc

        P.wait("tensor", sVE, b_ve)
        t1o_evac = {}
        for blk in range(4):
            r0 = 32 * (blk % 2)
            if blk >= 2:
                P.wait("tensor", sVE, t1o_evac[blk - 2])
            for oc in range(8):
                P.op("tensor", lambda e, blk=blk, oc=oc, r0=r0: e.matmul(
                    ps_pv[r0:r0 + 16, :], aot_s[:, oc * 16:(oc + 1) * 16],
                    numt[:, oc * 2048 + blk * 512: oc * 2048 + (blk + 1) * 512],
                    start=(oc == 0), stop=(oc == 7),
                    tile_position=(0, r0)), inc=(sPE if oc == 7 else None))
            P.wait("vector", sPE, P.val(sPE))
            P.op("vector", lambda e, blk=blk, r0=r0: e.tensor_scalar_mul(
                t1o[0:16, blk * 512:(blk + 1) * 512], ps_pv[r0:r0 + 16, :],
                1.0 / 16.0), inc=sVE)
            t1o_evac[blk] = P.val(sVE)
        t1o_ve = P.val(sVE)

        xf_ld = {}
        xbo_ve = {}

        def emit_xf(tt):
            if tt >= 2:
                P.wait("sync", sVE, xbo_ve[tt - 2])
            else:
                P.wait("sync", sPE, pair_done_pe[7])
            P.op("sync", lambda e, tt=tt: e.dma_start(
                out=xf2[tt % 2][:, :], in_=x_r[:, tt, :]), inc=sLd, inck=16)
            xf_ld[tt] = P.val(sLd)

        emit_xf(0)
        emit_xf(1)

        if stage == "C0":
            dummy_out()
            finish_emit()
            return nc

        P.wait("tensor", sCa, wot_ca)
        P.wait("tensor", sVE, t1o_ve)
        o_evac = {}
        st_done = {}
        ntiles = 1 if stage == "C1" else 16
        for tt in range(ntiles):
            P.wait("vector", sLd, max(xf_ld[tt], bo_ld))
            P.op("vector", lambda e, tt=tt: e.tensor_add(
                xbo[:, :], xf2[tt % 2][:, :], bo_sb[:, :]), inc=sVE)
            xbo_ve[tt] = P.val(sVE)
            for dh in range(2):
                fps = ps_s1[:, dh * 512:(dh + 1) * 512]
                ops_ = ps_s0[:, dh * 512:(dh + 1) * 512]
                P.wait("tensor", sVE, o_evac.get((tt - 1, dh), 0))
                for ic in range(8):
                    P.op("tensor", lambda e, ic=ic, tt=tt, dh=dh, fps=fps: e.matmul(
                        fps, qlt[:, ic * 2048 + tt * 128: ic * 2048 + (tt + 1) * 128],
                        qrh[:, ic * 1024 + dh * 512: ic * 1024 + (dh + 1) * 512],
                        start=(ic == 0), stop=(ic == 7)))
                for oc in range(8):
                    P.op("tensor", lambda e, oc=oc, tt=tt, dh=dh, ops_=ops_: e.matmul(
                        ops_, numt[:, oc * 2048 + tt * 128: oc * 2048 + (tt + 1) * 128],
                        wot_v[:, oc * 1024 + dh * 512: oc * 1024 + (dh + 1) * 512],
                        start=(oc == 0), stop=False))
                P.op("tensor", lambda e, tt=tt, dh=dh, ops_=ops_: e.matmul(
                    ops_, t1o[0:16, tt * 128:(tt + 1) * 128],
                    bot_s[0:16, dh * 512:(dh + 1) * 512],
                    start=False, stop=True), inc=sPE)
                mm_pe = P.val(sPE)
                P.wait("vector", sPE, mm_pe)
                P.wait("vector", sVE, xbo_ve[tt])
                P.op("vector", lambda e, tt=tt, dh=dh, fps=fps: e.scalar_tensor_tensor(
                    fscr[:, :], fps, ls_all[:, tt:tt + 1],
                    rs_full[:, dh * 512:(dh + 1) * 512], AL.mult, AL.mult), inc=sVE)
                P.op("vector", lambda e, dh=dh, ops_=ops_: e.scalar_tensor_tensor(
                    oscr[:, :], ops_, 1.0, xbo[:, dh * 512:(dh + 1) * 512],
                    AL.bypass, AL.add), inc=sVE)
                o_evac[(tt, dh)] = P.val(sVE)
                P.wait("vector", sVE, P.val(sVE))
                ob = outst[dh]
                P.wait("vector", sSt, st_done.get((tt - 1, dh), 0))
                P.op("vector", lambda e, ob=ob: e.tensor_add(ob[:, :], fscr[:, :], oscr[:, :]), inc=sVE)
                P.wait("sync", sVE, P.val(sVE))
                P.op("sync", lambda e, tt=tt, dh=dh, ob=ob: e.dma_start(
                    out=out_r[:, tt, dh * 512:(dh + 1) * 512], in_=ob[:, :]), inc=sSt, inck=16)
                st_done[(tt, dh)] = P.val(sSt)
            if tt + 2 < 16:
                emit_xf(tt + 2)

        if stage == "C1":
            dummy_out()
            finish_emit()
            return nc

        P.wait("sync", sSt, P.val(sSt))

        finish_emit()

    return nc


def _stack_a(aq, ak, av):
    a96 = np.zeros((96, D), dtype=np.float32)
    a96[0:16] = aq
    a96[32:48] = ak
    a96[64:80] = av
    return a96.T


def kernel(**inputs):
    x = np.asarray(inputs["L0_attn_x"])
    wq = np.asarray(inputs["L0_attn_Wq"])
    wk = np.asarray(inputs["L0_attn_Wk"])
    wv = np.asarray(inputs["L0_attn_Wv"])
    wo = np.asarray(inputs["L0_attn_Wo"])
    bo = np.asarray(inputs["L0_attn_bo"])
    aq = np.asarray(inputs["L0_attn_Aq"])
    bq = np.asarray(inputs["L0_attn_Bq"])
    ak = np.asarray(inputs["L0_attn_Ak"])
    bk = np.asarray(inputs["L0_attn_Bk"])
    av = np.asarray(inputs["L0_attn_Av"])
    bv = np.asarray(inputs["L0_attn_Bv"])
    ao = np.asarray(inputs["L0_attn_Ao"])
    bo_l = np.asarray(inputs["L0_attn_Bo"])
    rhs = np.asarray(inputs["L0_ffn_rhs"])

    C = np.ascontiguousarray
    shared = {
        "wqt": C(wq.T), "wkt": C(wk.T), "wvt": C(wv.T), "wot": C(wo.T),
        "aqkvt": C(_stack_a(aq, ak, av)), "aot": C(ao.T),
        "bqt": C(bq.T), "bkt": C(bk.T), "bvt": C(bv.T), "bot": C(bo_l.T),
        "rhst": C(rhs.T),
        "bo_full": C(np.broadcast_to(bo, (128, D)).copy()),
    }

    in_maps = []
    for c in range(8):
        b, half = c // 2, c % 2
        own = x[b, half * 2048:(half + 1) * 2048]
        oth = x[b, (1 - half) * 2048:(2 - half) * 2048]
        m = dict(shared)
        xcat = np.concatenate([own, oth], axis=0)
        m["x"] = C(xcat.reshape(32, 128, D).transpose(1, 0, 2).reshape(NSEQ, D))
        in_maps.append(m)

    nc = build_nc()
    res = run_bass_kernel_spmd(nc, in_maps, core_ids=list(range(8)))

    out = np.empty((4, NSEQ, D), dtype=np.float32)
    for c in range(8):
        b, half = c // 2, c % 2
        out[b, half * 2048:(half + 1) * 2048] = res.results[c]["out"]
    return out


# version 40
# speedup vs baseline: 1.0788x; 1.0116x over previous
import numpy as np
import ml_dtypes
import concourse.bass as bass
import concourse.mybir as mybir
from concourse.bass_utils import run_bass_kernel_spmd

F32 = mybir.dt.float32
BF16 = mybir.dt.bfloat16
BF = ml_dtypes.bfloat16
AL = mybir.AluOpType
ACTF = mybir.ActivationFunctionType
AXX = mybir.AxisListType.X

D = 1024
NSEQ = 4096
TOWN = 2048
GROUPS = 8
OWN_GROUPS = 4
PAIRS = 8
KCH = 32
QB = 4
CR = float(np.float32(1.5 * 2 ** 23))
EPS = 1e-5
MHAT = 25.0


class Plan:
    ENGS = ("sync", "gpsimd", "tensor", "vector", "scalar")

    def __init__(self):
        self.q = {e: [] for e in self.ENGS}
        self.n = {}

    def inc(self, sem, k=1):
        self.n[sem] = self.n.get(sem, 0) + k
        return self.n[sem]

    def val(self, sem):
        return self.n.get(sem, 0)

    def add(self, eng, fn):
        self.q[eng].append(fn)

    def wait(self, eng, sem, v):
        if v > 0:
            self.add(eng, lambda e, s=sem, vv=v: e.wait_ge(s, vv))

    def op(self, eng, build, inc=None, inck=1):
        if inc is not None:
            self.inc(inc, inck)
            self.add(eng, lambda e, b=build, s=inc, k=inck: b(e).then_inc(s, k))
        else:
            self.add(eng, lambda e, b=build: b(e))


def build_nc(stage="full"):
    nc = bass.Bass("TRN2", target_bir_lowering=False, debug=False)

    x_d = nc.dram_tensor("x", [NSEQ, D], F32, kind="ExternalInput").ap()
    wqt_d = nc.dram_tensor("wqt", [D, D], F32, kind="ExternalInput").ap()
    wkt_d = nc.dram_tensor("wkt", [D, D], F32, kind="ExternalInput").ap()
    wvt_d = nc.dram_tensor("wvt", [D, D], F32, kind="ExternalInput").ap()
    wot_d = nc.dram_tensor("wot", [D, D], F32, kind="ExternalInput").ap()
    aqkv_d = nc.dram_tensor("aqkvt", [D, 96], F32, kind="ExternalInput").ap()
    aot_d = nc.dram_tensor("aot", [D, 16], F32, kind="ExternalInput").ap()
    bqt_d = nc.dram_tensor("bqt", [16, D], F32, kind="ExternalInput").ap()
    bkt_d = nc.dram_tensor("bkt", [16, D], F32, kind="ExternalInput").ap()
    bvt_d = nc.dram_tensor("bvt", [16, D], F32, kind="ExternalInput").ap()
    bot_d = nc.dram_tensor("bot", [16, D], F32, kind="ExternalInput").ap()
    rhst_d = nc.dram_tensor("rhst", [D, D], F32, kind="ExternalInput").ap()
    bof_d = nc.dram_tensor("bo_full", [128, D], F32, kind="ExternalInput").ap()
    out_d = nc.dram_tensor("out", [TOWN, D], F32, kind="ExternalOutput").ap()

    kt_h = nc.dram_tensor("kt_h", [D, NSEQ], BF16, kind="Internal").ap()
    v_h = nc.dram_tensor("v_h", [NSEQ, D], BF16, kind="Internal").ap()
    l_h = nc.dram_tensor("l_h", [2, 512], F32, kind="Internal").ap()

    ident_d = nc.inline_tensor(np.eye(128, dtype=np.float32), name="ident")

    x_r = x_d.rearrange("(p a) d -> p a d", a=32)
    v_h_r = v_h.rearrange("(a p) o -> p a o", p=128)
    out_r = out_d.rearrange("(a p) d -> p a d", p=128)

    UW = 27648

    from contextlib import ExitStack
    ctx = ExitStack()
    sb = lambda *a: ctx.enter_context(nc.sbuf_tensor(*a))
    pst = lambda *a: ctx.enter_context(nc.psum_tensor(*a))
    sem = lambda n: ctx.enter_context(nc.semaphore(n))
    with ctx:
        qt = sb("qt", [128, 8 * 2048], BF16)
        qlt = sb("qlt", [128, 8 * 2048], BF16)
        qrh = sb("qrh", [128, 8 * 1024], BF16)
        rs_full = sb("rs_full", [128, 1024], F32)
        ls_all = sb("ls_all", [128, 16], F32)
        stats = sb("stats", [128, 160], F32)
        ones_b = sb("ones_b", [128, 64], BF16)
        ones_f = sb("ones_f", [128, 128], F32)
        ident_f = sb("ident_f", [128, 128], F32)
        biasx = sb("biasx", [128, 4], F32)
        aqkv_s = sb("aqkv_s", [128, 768], BF16)
        aot_s = sb("aot_s", [128, 128], BF16)
        bqt_s = sb("bqt_s", [128, 1024], BF16)
        bkt_s = sb("bkt_s", [128, 1024], BF16)
        bvt_s = sb("bvt_s", [128, 1024], BF16)
        bot_s = sb("bot_s", [128, 1024], BF16)
        t1s = sb("t1s", [128, 512], BF16)
        U = sb("U", [128, UW], F32)
        ps_s0 = pst("ps_s0", [128, 1024], F32)
        ps_s1 = pst("ps_s1", [128, 1024], F32)
        ps_s2 = pst("ps_s2", [128, 1024], F32)
        ps_pv = pst("ps_pv", [128, 512], F32)
        ps_l = pst("ps_l", [128, 512], F32)
        sLd = sem("sLd")
        sCa = sem("sCa")
        sSt = sem("sSt")
        sPE = sem("sPE")
        sVE = sem("sVE")
        sAC = sem("sAC")
        block = ctx.enter_context(nc.Block())

        P = Plan()

        def uf(off, n):
            ap = U[:, off:off + n]
            return ap

        def ui(off, n):
            ap = U[:, off:off + n].bitcast(mybir.dt.int32)
            return ap

        def ub(off, nbf):
            ap = U[:, off:off + nbf // 2].bitcast(BF16)
            assert tuple(ap.shape) == (128, nbf), ap.shape
            return ap

        P.op("sync", lambda e: e.dma_start(out=ident_f[:, :], in_=ident_d.ap()), inc=sLd, inck=16)
        P.op("vector", lambda e: e.memset(biasx[:, :], -MHAT))
        P.op("vector", lambda e: e.memset(ones_b[:, :], 1.0))
        P.op("vector", lambda e: e.memset(ones_f[:, :], 1.0), inc=sVE)
        init_ve = P.val(sVE)
        P.wait("scalar", sVE, init_ve)
        P.op("scalar", lambda e: e.activation(
            stats[:, 156:157], ones_f[:, 0:1], ACTF.Sqrt), inc=sAC)
        rhst_v = uf(4096, 8192)
        rhst_state = {}

        wq_v = ub(0, 8192)
        wk_v = ub(4096, 8192)
        wv_v = ub(8192, 8192)
        xg = [uf(12288, 4096), uf(16384, 4096)]
        xnT2 = [ub(20480, 4096), ub(22528, 4096)]
        qs2b = [uf(24576, 1024), uf(25600, 1024)]
        kvst = [ub(26624, 512), ub(26880, 512), ub(27136, 512), ub(27392, 512)]

        wload = {}

        def emit_lora_loads():
            P.op("gpsimd", lambda e: e.dma_start(
                out=aqkv_s[:, :].rearrange("p (c r) -> p c r", c=8),
                in_=aqkv_d.rearrange("(c p) r -> p c r", p=128)), inc=sCa, inck=16)
            P.op("gpsimd", lambda e: e.dma_start(
                out=aot_s[:, :].rearrange("p (c r) -> p c r", c=8),
                in_=aot_d.rearrange("(c p) r -> p c r", p=128)), inc=sCa, inck=16)
            for dst, src_, r0 in ((bqt_s, bqt_d, 0), (bkt_s, bkt_d, 32),
                                  (bvt_s, bvt_d, 64), (bot_s, bot_d, 0)):
                P.op("gpsimd", lambda e, d=dst, s=src_, r0=r0: e.dma_start(
                    out=d[r0:r0 + 16, :], in_=s), inc=sCa, inck=16)
            return P.val(sCa)

        def emit_heavy_wloads():
            for nm, dst, src_, tpj in (("k", wk_v, wkt_d, 3), ("v", wv_v, wvt_d, 7)):
                P.wait("gpsimd", sPE, a1_tp[tpj])
                P.op("gpsimd", lambda e, d=dst, s=src_: e.dma_start(
                    out=d.rearrange("p (c o) -> p c o", c=8),
                    in_=s.rearrange("(c p) o -> p c o", p=128)), inc=sCa, inck=16)
                wload[nm] = P.val(sCa)

        xload_ld = {}
        xload_ca = {}
        xg_free = {}

        def emit_xload(g):
            if g >= 2:
                pe_v, ve_v = xg_free[g - 2]
                P.wait("sync", sPE, pe_v)
                P.wait("sync", sVE, ve_v)
            if g == 0:
                P.op("sync", lambda e, g=g: e.dma_start(
                    out=xg[g % 2][:, 0:2048].rearrange("p (a d) -> p a d", a=2),
                    in_=x_r[:, 4 * g:4 * g + 2, :]), inc=sLd, inck=16)
                P.op("gpsimd", lambda e, g=g: e.dma_start(
                    out=xg[g % 2][:, 2048:4096].rearrange("p (a d) -> p a d", a=2),
                    in_=x_r[:, 4 * g + 2:4 * g + 4, :]), inc=sCa, inck=16)
                xload_ca[g] = P.val(sCa)
            else:
                P.op("sync", lambda e, g=g: e.dma_start(
                    out=xg[g % 2].rearrange("p (a d) -> p a d", a=4),
                    in_=x_r[:, 4 * g:4 * g + 4, :]), inc=sLd, inck=16)
            xload_ld[g] = P.val(sLd)

        emit_xload(0)
        rhst_r = rhst_d.rearrange("(c p) i -> p c i", p=128)
        P.op("gpsimd", lambda e: e.dma_start(
            out=rhst_v[:, 0:4096].rearrange("p (c i) -> p c i", c=4),
            in_=rhst_r[:, 0:4, :]), inc=sCa, inck=16)
        rhst_state["ca"] = P.val(sCa)
        P.op("sync", lambda e: e.dma_start(
            out=rhst_v[:, 4096:8192].rearrange("p (c i) -> p c i", c=4),
            in_=rhst_r[:, 4:8, :]), inc=sLd, inck=16)
        rhst_state["ld"] = P.val(sLd)
        emit_xload(1)
        lora_ca = emit_lora_loads()
        P.op("gpsimd", lambda e: e.dma_start(
            out=wq_v.rearrange("p (c o) -> p c o", c=8),
            in_=wqt_d.rearrange("(c p) o -> p c o", p=128)), inc=sCa, inck=16)
        wload["q"] = P.val(sCa)

        ps_slots = [ps_s0[:, 0:512], ps_s0[:, 512:1024], ps_s1[:, 0:512], ps_s1[:, 512:1024]]
        slot_evac = [0, 0, 0, 0]
        slot_i = [0]
        stage_hist = []
        k_st = {}
        v_st = {}
        s2_free_ac = [0]

        def proj_tile(mms, evac_fn, fold_gate=None):
            si = slot_i[0] % 4
            slot_i[0] += 1
            ps = ps_slots[si]
            P.wait("tensor", sAC, slot_evac[si])
            n = len(mms)
            for i, mm in enumerate(mms):
                if fold_gate is not None and i == n - 1:
                    P.wait("tensor", fold_gate[0], fold_gate[1])
                lh, rh = mm[0], mm[1]
                tp = mm[2] if len(mm) > 2 else None
                P.op("tensor", lambda e, lh=lh, rh=rh, i=i, ps=ps, n=n, tp=tp:
                     e.matmul(ps, lh, rh, start=(i == 0), stop=(i == n - 1),
                              tile_position=tp),
                     inc=(sPE if i == n - 1 else None))
            P.wait("scalar", sPE, P.val(sPE))
            P.op("scalar", lambda e, ps=ps: evac_fn(e, ps), inc=sAC)
            slot_evac[si] = P.val(sAC)

        last_t1 = [0]

        def lora_t1(xnT, ready_ac):
            P.wait("tensor", sCa, lora_ca)
            P.wait("tensor", sAC, ready_ac)
            P.wait("tensor", sAC, last_t1[0])
            for ic in range(8):
                P.op("tensor", lambda e, ic=ic, xnT=xnT: e.matmul(
                    ps_pv[0:96, :], aqkv_s[:, ic * 96:(ic + 1) * 96],
                    xnT[:, ic * 512:(ic + 1) * 512],
                    start=(ic == 0), stop=(ic == 7)), inc=(sPE if ic == 7 else None))
            P.wait("scalar", sPE, P.val(sPE))
            P.op("scalar", lambda e: e.mul(t1s[0:96, :], ps_pv[0:96, :], 1.0 / 16.0), inc=sAC)
            last_t1[0] = P.val(sAC)
            return P.val(sAC)

        def kv_stage_store():
            idx = len(stage_hist)
            st = kvst[idx % 4]
            wait_st = stage_hist[idx - 4] if idx >= 4 else 0
            P.wait("scalar", sSt, wait_st)

            def evac(e, ps, st=st):
                return e.copy(st[:, :], ps)
            return st, evac

        s2_half = [0, 0]
        s2_par = [0]

        def transpose_batch(src_ap, dst_fn, in_ve):
            P.wait("tensor", sVE, in_ve)
            tp = 0
            for h in range(2):
                par = s2_par[0]
                s2_par[0] ^= 1
                P.wait("tensor", sAC, s2_half[par])
                for ic in range(4):
                    icg = h * 4 + ic
                    P.op("tensor", lambda e, icg=icg, ic=ic, s=src_ap, par=par: e.transpose(
                        ps_s2[:, par * 512 + ic * 128: par * 512 + (ic + 1) * 128],
                        s[:, icg * 128:(icg + 1) * 128],
                        ident_f[:, :]), inc=(sPE if ic == 3 else None))
                tp = P.val(sPE)
                P.wait("scalar", sPE, tp)
                P.op("scalar", lambda e, h=h, par=par: dst_fn(
                    e, h, ps_s2[:, par * 512:(par + 1) * 512]), inc=sAC)
                s2_half[par] = P.val(sAC)
            return tp

        a1_state = {}
        a1_tp = {}

        def emit_a1():
            for hh in range(2):
                if hh == 0:
                    P.wait("vector", sCa, rhst_state["ca"])
                else:
                    P.wait("vector", sLd, rhst_state["ld"])
                c0 = hh * 4
                P.op("vector", lambda e, c0=c0, hh=hh: e.reduce_max(
                    stats[:, c0:c0 + 4],
                    rhst_v[:, hh * 4096:(hh + 1) * 4096].rearrange("p (c i) -> p c i", c=4),
                    axis=AXX, apply_absolute_value=True), inc=sVE)
                P.wait("vector", sVE, P.val(sVE))
                P.op("vector", lambda e, c0=c0: e.tensor_scalar_max(
                    stats[:, 8 + c0:8 + c0 + 4], stats[:, c0:c0 + 4], 1e-12), inc=sVE)
                P.wait("vector", sVE, P.val(sVE))
                P.op("vector", lambda e, c0=c0: e.reciprocal(
                    stats[:, 16 + c0:16 + c0 + 4], stats[:, 8 + c0:8 + c0 + 4]), inc=sVE)
                P.wait("vector", sVE, P.val(sVE))
                P.op("vector", lambda e, c0=c0: e.tensor_scalar_mul(
                    stats[:, 24 + c0:24 + c0 + 4], stats[:, 16 + c0:16 + c0 + 4], 127.0))
                P.op("vector", lambda e, c0=c0: e.tensor_scalar_mul(
                    stats[:, 32 + c0:32 + c0 + 4], stats[:, 8 + c0:8 + c0 + 4],
                    1.0 / 127.0), inc=sVE)
                a1_state["sc_ve%d" % hh] = P.val(sVE)
                P.wait("vector", sVE, P.val(sVE))

            def emit_j(j):
                sl = rhst_v[:, j * 1024:(j + 1) * 1024]
                sli = sl.bitcast(mybir.dt.int32)
                P.op("vector", lambda e, j=j, sli=sli, sl=sl: e.tensor_scalar_mul(
                    sli, sl, stats[:, 24 + j:25 + j]), inc=sVE)
                P.wait("vector", sVE, P.val(sVE))
                P.op("vector", lambda e, sl=sl, sli=sli: e.tensor_copy(sl, sli), inc=sVE)
                qv = P.val(sVE)

                def evac_rh(e, h, psh, j=j):
                    return e.copy(
                        qrh.rearrange("p (c o) -> p c o", c=8)[:, 4 * h:4 * h + 4,
                                                               j * 128:(j + 1) * 128],
                        psh.rearrange("p (c t) -> p c t", c=4))
                a1_tp[j] = transpose_batch(sl, evac_rh, qv)

            rs_row = uf(26624, 1024)
            rst8 = uf(26752, 128)
            for j in range(2):
                emit_j(j)
            par = s2_par[0]
            s2_par[0] ^= 1
            P.wait("tensor", sAC, s2_half[par])
            P.wait("tensor", sVE, a1_state["sc_ve1"])
            P.op("tensor", lambda e, par=par: e.transpose(
                ps_s2[0:8, par * 512:par * 512 + 128], stats[:, 32:40],
                ident_f[:, :]), inc=sPE)
            rs_tp = P.val(sPE)
            P.wait("scalar", sPE, rs_tp)
            P.op("scalar", lambda e, par=par: e.copy(
                rst8[0:8, 0:128], ps_s2[0:8, par * 512:par * 512 + 128]), inc=sAC)
            s2_half[par] = P.val(sAC)
            P.wait("sync", sAC, P.val(sAC))
            for j in range(8):
                P.op("sync", lambda e, j=j: e.dma_start(
                    out=rs_row[0:1, j * 128:(j + 1) * 128], in_=rst8[j:j + 1, 0:128]),
                    inc=sSt, inck=16)
            rs_st = P.val(sSt)
            for j in range(2, 4):
                emit_j(j)
            P.wait("tensor", sSt, rs_st)
            P.wait("tensor", sAC, s2_half[0])
            P.wait("tensor", sAC, s2_half[1])
            P.op("tensor", lambda e: e.matmul(ps_s2[:, 0:512], ones_f[0:1, 0:128],
                                              rs_row[0:1, 0:512], start=True, stop=True))
            P.op("tensor", lambda e: e.matmul(ps_s2[:, 512:1024], ones_f[0:1, 0:128],
                                              rs_row[0:1, 512:1024], start=True, stop=True), inc=sPE)
            P.wait("scalar", sPE, P.val(sPE))
            P.op("scalar", lambda e: e.copy(rs_full[:, :], ps_s2[:, :]), inc=sAC)
            s2_half[0] = P.val(sAC)
            s2_half[1] = P.val(sAC)
            for j in range(4, 8):
                emit_j(j)

        qtrans_pe = {}
        qs2b_guard = [0]

        def sumsq(tt, xgv, b0):
            for hv in range(2):
                P.op("vector", lambda e, tt=tt, hv=hv, xgv=xgv, b0=b0: e.scalar_tensor_tensor(
                    ps_l[:, :], xgv[:, tt * 1024 + hv * 512:tt * 1024 + (hv + 1) * 512],
                    1.0, xgv[:, tt * 1024 + hv * 512:tt * 1024 + (hv + 1) * 512],
                    AL.bypass, AL.mult,
                    accum_out=stats[:, b0 + 48 + 2 * tt + hv:b0 + 49 + 2 * tt + hv]))

        def fe_stats(g):
            b0 = 40 + 56 * (g % 2)
            xgv = xg[g % 2]
            if g in xload_ca:
                P.wait("vector", sLd, xload_ld[g])
                for tt in range(2):
                    sumsq(tt, xgv, b0)
                P.op("vector", lambda e, xgv=xgv, b0=b0: e.reduce_sum(
                    stats[:, b0 + 4:b0 + 6],
                    xgv[:, 0:2048].rearrange("p (a d) -> p a d", a=2), axis=AXX))
                P.wait("vector", sCa, xload_ca[g])
                for tt in range(2, 4):
                    sumsq(tt, xgv, b0)
                P.op("vector", lambda e, xgv=xgv, b0=b0: e.reduce_sum(
                    stats[:, b0 + 6:b0 + 8],
                    xgv[:, 2048:4096].rearrange("p (a d) -> p a d", a=2), axis=AXX), inc=sVE)
            else:
                P.wait("vector", sLd, xload_ld[g])
                for tt in range(4):
                    sumsq(tt, xgv, b0)
                P.op("vector", lambda e, xgv=xgv, b0=b0: e.reduce_sum(
                    stats[:, b0 + 4:b0 + 8], xgv.rearrange("p (a d) -> p a d", a=4), axis=AXX), inc=sVE)
            P.wait("vector", sVE, P.val(sVE))
            P.op("vector", lambda e, b0=b0: e.tensor_add(
                stats[:, b0:b0 + 4], stats[:, b0 + 48:b0 + 56:2],
                stats[:, b0 + 49:b0 + 56:2]), inc=sVE)
            P.wait("vector", sVE, P.val(sVE))
            P.op("vector", lambda e, b0=b0: e.tensor_scalar_mul(
                stats[:, b0 + 8:b0 + 12], stats[:, b0 + 4:b0 + 8], 1.0 / D))
            P.op("vector", lambda e, b0=b0: e.tensor_scalar(
                stats[:, b0 + 12:b0 + 16], stats[:, b0:b0 + 4], 1.0 / D, EPS,
                AL.mult, AL.add), inc=sVE)
            P.wait("vector", sVE, P.val(sVE))
            P.op("vector", lambda e, b0=b0: e.tensor_mul(
                stats[:, b0 + 16:b0 + 20], stats[:, b0 + 8:b0 + 12], stats[:, b0 + 8:b0 + 12]), inc=sVE)
            P.wait("vector", sVE, P.val(sVE))
            P.op("vector", lambda e, b0=b0: e.tensor_sub(
                stats[:, b0 + 20:b0 + 24], stats[:, b0 + 12:b0 + 16], stats[:, b0 + 16:b0 + 20]), inc=sVE)
            P.wait("vector", sVE, P.val(sVE))
            P.op("vector", lambda e, b0=b0: e.reciprocal(
                stats[:, b0 + 24:b0 + 28], stats[:, b0 + 20:b0 + 24]), inc=sVE)
            P.wait("scalar", sVE, P.val(sVE))
            P.op("scalar", lambda e, b0=b0: e.activation(
                stats[:, b0 + 28:b0 + 32], stats[:, b0 + 24:b0 + 28], ACTF.Sqrt), inc=sAC)
            P.wait("vector", sAC, P.val(sAC))
            P.op("vector", lambda e, b0=b0: e.tensor_mul(
                stats[:, b0 + 32:b0 + 36], stats[:, b0 + 28:b0 + 32], stats[:, b0 + 28:b0 + 32]), inc=sVE)
            P.wait("vector", sVE, P.val(sVE))
            P.op("vector", lambda e, b0=b0: e.tensor_mul(
                stats[:, b0 + 36:b0 + 40], stats[:, b0 + 32:b0 + 36], stats[:, b0 + 20:b0 + 24]), inc=sVE)
            P.wait("vector", sVE, P.val(sVE))
            P.op("vector", lambda e, b0=b0: e.tensor_scalar(
                stats[:, b0 + 40:b0 + 44], stats[:, b0 + 36:b0 + 40], -0.5, 1.5,
                AL.mult, AL.add), inc=sVE)
            P.wait("vector", sVE, P.val(sVE))
            P.op("vector", lambda e, b0=b0: e.tensor_mul(
                stats[:, b0 + 44:b0 + 48], stats[:, b0 + 28:b0 + 32], stats[:, b0 + 40:b0 + 44]), inc=sVE)
            P.wait("vector", sVE, P.val(sVE))

        fe_lasttp = {}
        xn_ve_of = {}

        def fe_xn(g):
            b0 = 40 + 56 * (g % 2)
            xgv = xg[g % 2]
            xn_ve = {}
            for tt in range(4):
                P.op("vector", lambda e, tt=tt, xgv=xgv, b0=b0: e.tensor_scalar(
                    xgv[:, tt * 1024:(tt + 1) * 1024], xgv[:, tt * 1024:(tt + 1) * 1024],
                    stats[:, b0 + 8 + tt:b0 + 9 + tt], stats[:, b0 + 44 + tt:b0 + 45 + tt],
                    AL.subtract, AL.mult), inc=sVE)
                xn_ve[tt] = P.val(sVE)
            xn_ve_of[g] = xn_ve

        def frontend_xnT(g):
            xnT = xnT2[g % 2]
            xgv = xg[g % 2]
            xn_ve = xn_ve_of[g]
            last_tp = 0
            for tt in range(4):
                def evac_xn(e, h, psh, tt=tt, xnT=xnT):
                    return e.copy(
                        xnT.rearrange("p (c t) -> p c t", c=8)[:, 4 * h:4 * h + 4,
                                                               tt * 128:(tt + 1) * 128],
                        psh.rearrange("p (c t) -> p c t", c=4))
                last_tp = transpose_batch(xgv[:, tt * 1024:(tt + 1) * 1024], evac_xn, xn_ve[tt])
            xnT_ac_of[g] = P.val(sAC)
            fe_lasttp[g] = last_tp

        def frontend_quant(g):
            own = g < OWN_GROUPS
            xgv = xg[g % 2]
            quant_ve = 0
            if own:
                for tt in range(4):
                    gt = g * 4 + tt
                    qsb = qs2b[tt % 2]
                    if tt >= 2:
                        P.wait("vector", sPE, qtrans_pe[(g, tt - 2)])
                    P.op("vector", lambda e, tt=tt, xgv=xgv: e.reduce_max(
                        stats[:, 152:153], xgv[:, tt * 1024:(tt + 1) * 1024], axis=AXX,
                        apply_absolute_value=True), inc=sVE)
                    P.wait("vector", sVE, P.val(sVE))
                    P.op("vector", lambda e: e.tensor_scalar_max(
                        stats[:, 153:154], stats[:, 152:153], 1e-12), inc=sVE)
                    P.wait("vector", sVE, P.val(sVE))
                    P.op("vector", lambda e: e.reciprocal(stats[:, 154:155], stats[:, 153:154]), inc=sVE)
                    P.wait("vector", sVE, P.val(sVE))
                    P.op("vector", lambda e: e.tensor_scalar_mul(stats[:, 155:156], stats[:, 154:155], 127.0))
                    P.op("vector", lambda e, gt=gt: e.tensor_scalar_mul(
                        ls_all[:, gt:gt + 1], stats[:, 153:154], 1.0 / 127.0), inc=sVE)
                    P.wait("vector", sVE, P.val(sVE))
                    qsbi = qsb.bitcast(mybir.dt.int32)
                    P.op("vector", lambda e, tt=tt, xgv=xgv, qsbi=qsbi: e.tensor_scalar_mul(
                        qsbi, xgv[:, tt * 1024:(tt + 1) * 1024],
                        stats[:, 155:156]), inc=sVE)
                    P.wait("vector", sVE, P.val(sVE))
                    P.op("vector", lambda e, qsb=qsb, qsbi=qsbi: e.tensor_copy(qsb[:, :], qsbi), inc=sVE)
                    qv = P.val(sVE)

                    def evac_q(e, h, psh, gt=gt):
                        return e.copy(
                            qlt.rearrange("p (c t) -> p c t", c=8)[:, 4 * h:4 * h + 4,
                                                                   gt * 128:(gt + 1) * 128],
                            psh.rearrange("p (c t) -> p c t", c=4))
                    qtrans_pe[(g, tt)] = transpose_batch(qsb, evac_q, qv)
                qs2b_guard[0] = qtrans_pe[(g, 3)]
                quant_ve = P.val(sVE)
            xg_free[g] = (qtrans_pe[(g, 3)] if own else fe_lasttp[g], quant_ve)
            if g + 2 < GROUPS:
                emit_xload(g + 2)

        def frontend(g):
            frontend_xnT(g)
            frontend_quant(g)


        def backend(g, t1_ac):
            own = g < OWN_GROUPS
            xnT = xnT2[g % 2]
            P.wait("tensor", sCa, wload["k"])
            for oc in range(8):
                mms = [(wk_v[:, ic * 1024 + oc * 128: ic * 1024 + (oc + 1) * 128],
                        xnT[:, ic * 512:(ic + 1) * 512]) for ic in range(8)]
                mms.append((bkt_s[32:48, oc * 128:(oc + 1) * 128], t1s[32:48, :], (32, 0)))
                st, evac = kv_stage_store()
                proj_tile(mms, evac, fold_gate=(sAC, t1_ac) if oc == 0 else None)
                P.wait("sync", sAC, P.val(sAC))
                P.op("sync", lambda e, oc=oc, g=g, st=st: e.dma_start(
                    out=kt_h[oc * 128:(oc + 1) * 128, g * 512:(g + 1) * 512],
                    in_=st[:, :]), inc=sSt, inck=16)
                stage_hist.append(P.val(sSt))
                k_st[(g, oc)] = P.val(sSt)

            P.wait("tensor", sCa, wload["v"])
            for oh in range(2):
                for tt in range(4):
                    mms = [(xnT[:, ic * 512 + tt * 128: ic * 512 + (tt + 1) * 128],
                            wv_v[:, ic * 1024 + oh * 512: ic * 1024 + (oh + 1) * 512])
                           for ic in range(8)]
                    mms.append((t1s[64:80, tt * 128:(tt + 1) * 128],
                                bvt_s[64:80, oh * 512:(oh + 1) * 512], (64, 0)))
                    st, evac = kv_stage_store()
                    proj_tile(mms, evac)
                    P.wait("sync", sAC, P.val(sAC))
                    P.op("sync", lambda e, g=g, tt=tt, oh=oh, st=st: e.dma_start(
                        out=v_h_r[:, g * 4 + tt, oh * 512:(oh + 1) * 512],
                        in_=st[:, :]), inc=sSt, inck=16)
                    stage_hist.append(P.val(sSt))
                    v_st[(g, tt, oh)] = P.val(sSt)

            if own:
                P.wait("tensor", sCa, wload["q"])
                for oc in range(8):
                    mms = [(wq_v[:, ic * 1024 + oc * 128: ic * 1024 + (oc + 1) * 128],
                            xnT[:, ic * 512:(ic + 1) * 512]) for ic in range(8)]
                    mms.append((bqt_s[0:16, oc * 128:(oc + 1) * 128], t1s[0:16, :]))

                    def evac_q2(e, ps, oc=oc, g=g):
                        return e.copy(
                            qt[:, oc * 2048 + g * 512: oc * 2048 + (g + 1) * 512], ps)
                    proj_tile(mms, evac_q2)


        xnT_ac_of = {}
        fe_stats(0)
        fe_xn(0)
        frontend_xnT(0)
        emit_a1()
        emit_heavy_wloads()
        frontend_quant(0)
        fe_stats(1)
        fe_xn(1)
        for g in range(GROUPS):
            t1_ac = lora_t1(xnT2[g % 2], xnT_ac_of[g])
            if g + 1 < GROUPS:
                frontend(g + 1)
            if g + 2 < GROUPS:
                fe_stats(g + 2)
                fe_xn(g + 2)
            backend(g, t1_ac)

        a2_st = P.val(sSt)
        a2_ve = P.val(sVE)
        a2_ac = P.val(sAC)

        def dummy_out():
            P.wait("sync", sVE, P.val(sVE))
            P.wait("sync", sPE, P.val(sPE))
            P.wait("sync", sAC, P.val(sAC))
            for tt in range(16):
                P.op("sync", lambda e, tt=tt: e.dma_start(
                    out=out_r[:, tt, :], in_=bo_sb[:, :]), inc=sSt, inck=16)
            P.wait("sync", sSt, P.val(sSt))

        def finish_emit():
            @block.sync
            def _(e):
                for fn in P.q["sync"]:
                    fn(e)

            @block.gpsimd
            def _(e):
                for fn in P.q["gpsimd"]:
                    fn(e)

            @block.tensor
            def _(e):
                for fn in P.q["tensor"]:
                    fn(e)

            @block.vector
            def _(e):
                for fn in P.q["vector"]:
                    fn(e)

            @block.scalar
            def _(e):
                for fn in P.q["scalar"]:
                    fn(e)

        if stage == "A":
            dummy_out()
            finish_emit()
            return nc

        ktb = [ub(0, 4096), ub(2048, 4096)]
        vsb = [ub(4096, 4096), ub(6144, 4096)]
        pbuf = [ub(8192, 1024), ub(8704, 1024), ub(9216, 1024)]
        l_sp = uf(10240, 512)
        rl_full = uf(10752, 512)
        numt = ub(11264, 16384)

        kv_load_ld = {}
        kt_ld = {}

        def emit_kv_load(p, after_pe):
            P.wait("gpsimd", sPE, after_pe)
            P.wait("gpsimd", sSt, k_st[(GROUPS - 1, p)])
            P.op("gpsimd", lambda e, p=p: e.dma_start(
                out=ktb[p % 2][:, :], in_=kt_h[p * 128:(p + 1) * 128, :]), inc=sCa, inck=16)
            kt_ld[p] = P.val(sCa)
            P.wait("gpsimd", sSt, v_st[(GROUPS - 1, 3, p // 4)])
            P.op("gpsimd", lambda e, p=p: e.dma_start(
                out=vsb[p % 2][:, :].rearrange("p (a o) -> p a o", a=32),
                in_=v_h_r[:, :, p * 128:(p + 1) * 128]), inc=sCa, inck=16)
            kv_load_ld[p] = P.val(sCa)

        emit_kv_load(0, 0)
        emit_kv_load(1, 0)

        P.wait("tensor", sVE, a2_ve)
        P.wait("tensor", sAC, a2_ac)
        P.wait("scalar", sVE, init_ve)

        pair_done_pe = {}
        sring = [ps_s0, ps_s1]
        PV_banks = [ps_pv, ps_s2[:, 0:512]]
        L_banks = [ps_l, ps_s2[:, 512:1024]]
        exp_of = {}
        pcons = {}
        scores_pe = {}
        dance_done = [0, 0]
        prev_repl_pe = [0]
        pending = [None]

        def flush_dance_pe():
            Lb, PVb, pi, p_, qb_, l_ve = pending[0]
            pending[0] = None
            P.wait("tensor", sVE, l_ve)
            P.op("tensor", lambda e, Lb=Lb: e.matmul(
                Lb[0:64, :], ones_f[0:1, 0:64], l_sp[0:1, :],
                start=True, stop=True, tile_position=(0, 0)))
            P.op("tensor", lambda e, Lb=Lb: e.matmul(
                Lb[64:128, :], ones_f[32:33, 0:64], l_sp[32:33, :],
                start=True, stop=True, tile_position=(32, 64)), inc=sPE)
            P.wait("vector", sPE, P.val(sPE))
            P.op("vector", lambda e, Lb=Lb: e.reciprocal(rl_full[:, :], Lb[:, :]), inc=sVE)
            P.wait("vector", sVE, P.val(sVE))
            P.op("vector", lambda e, p_=p_, qb_=qb_, PVb=PVb: e.tensor_mul(
                numt[:, p_ * 2048 + qb_ * 512: p_ * 2048 + (qb_ + 1) * 512],
                PVb[:, :], rl_full[:, :]), inc=sVE)
            dance_done[pi] = P.val(sVE)

        def flush_dance():
            Lb, PVb, pi, p_, qb_, l_ve = pending[0]
            pending[0] = None
            P.wait("gpsimd", sVE, l_ve)
            P.op("gpsimd", lambda e: e.dma_start(
                out=l_h[0:1, :], in_=l_sp[0:1, :]), inc=sCa, inck=16)
            P.op("gpsimd", lambda e: e.dma_start(
                out=l_h[1:2, :], in_=l_sp[32:33, :]), inc=sCa, inck=16)
            st_ca = P.val(sCa)
            P.wait("gpsimd", sCa, st_ca)
            P.op("gpsimd", lambda e: e.dma_start(
                out=rl_full[0:64, :],
                in_=l_h[0:1, :].broadcast_to([64, 512])), inc=sCa, inck=16)
            P.op("gpsimd", lambda e: e.dma_start(
                out=rl_full[64:128, :],
                in_=l_h[1:2, :].broadcast_to([64, 512])), inc=sCa, inck=16)
            prev_repl_pe[0] = st_ca
            P.wait("vector", sCa, P.val(sCa))
            P.op("vector", lambda e: e.reciprocal(rl_full[:, :], rl_full[:, :]), inc=sVE)
            P.wait("vector", sVE, P.val(sVE))
            P.op("vector", lambda e, p_=p_, qb_=qb_, PVb=PVb: e.tensor_mul(
                numt[:, p_ * 2048 + qb_ * 512: p_ * 2048 + (qb_ + 1) * 512],
                PVb[:, :], rl_full[:, :]), inc=sVE)
            dance_done[pi] = P.val(sVE)

        NCH = QB * KCH
        TOT = PAIRS * NCH

        def emit_scores(c):
            p_, j_ = divmod(c, NCH)
            qb_, kc_ = divmod(j_, KCH)
            if j_ == 0:
                P.wait("tensor", sCa, kt_ld[p_])
            kb = ktb[p_ % 2]
            S = sring[c % 2]
            if c >= 2 and (j_ < 2):
                P.wait("tensor", sAC, exp_of[c - 2])
            qa = qt[0:64, p_ * 2048 + qb_ * 512: p_ * 2048 + (qb_ + 1) * 512]
            qbb = qt[64:128, p_ * 2048 + qb_ * 512: p_ * 2048 + (qb_ + 1) * 512]
            P.op("tensor", lambda e, kc=kc_, qa=qa, S=S, kb=kb: e.matmul(
                S[:, 0:512], kb[0:64, kc * 128:(kc + 1) * 128], qa,
                start=True, stop=True, tile_position=(0, 0)))
            P.op("tensor", lambda e, kc=kc_, qbb=qbb, S=S, kb=kb: e.matmul(
                S[:, 512:1024], kb[64:128, kc * 128:(kc + 1) * 128], qbb,
                start=True, stop=True, tile_position=(64, 0)), inc=sPE)
            scores_pe[c] = P.val(sPE)

        emit_scores(0)
        emit_scores(1)
        for c in range(TOT):
            p, j = divmod(c, NCH)
            qb, kc = divmod(j, KCH)
            pi = qb % 2
            S = sring[c % 2]
            PVb = PV_banks[pi]
            Lb = L_banks[pi]
            vb = vsb[p % 2]
            P.wait("scalar", sPE, scores_pe[c])
            pb = pbuf[c % 3]
            P.op("scalar", lambda e, S=S, pb=pb: e.activation(
                pb[:, :], S[:, :], ACTF.Exp, bias=biasx[:, 0:1], scale=0.125), inc=sAC)
            exp_of[c] = P.val(sAC)
            P.wait("tensor", sAC, exp_of[c])
            if c + 2 < TOT:
                emit_scores(c + 2)
            st = kc == 0
            sp = kc == KCH - 1
            if j == 0:
                P.wait("tensor", sCa, kv_load_ld[p])
            if st:
                P.wait("tensor", sVE, dance_done[pi])
            P.op("tensor", lambda e, vb=vb, kc=kc, pb=pb, st=st, sp=sp, PVb=PVb: e.matmul(
                PVb[0:64, :], vb[:, kc * 128:kc * 128 + 64], pb[:, 0:512],
                start=st, stop=sp, tile_position=(0, 0)))
            P.op("tensor", lambda e, vb=vb, kc=kc, pb=pb, st=st, sp=sp, PVb=PVb: e.matmul(
                PVb[64:128, :], vb[:, kc * 128 + 64:kc * 128 + 128], pb[:, 512:1024],
                start=st, stop=sp, tile_position=(0, 64)))
            P.op("tensor", lambda e, pb=pb, st=st, sp=sp, Lb=Lb: e.matmul(
                Lb[0:1, :], ones_b[:, 0:1], pb[:, 0:512],
                start=st, stop=sp, tile_position=(0, 0)))
            P.op("tensor", lambda e, pb=pb, st=st, sp=sp, Lb=Lb: e.matmul(
                Lb[32:33, :], ones_b[:, 0:1], pb[:, 512:1024],
                start=st, stop=sp, tile_position=(0, 32)), inc=sPE)
            pcons[c] = P.val(sPE)
            if kc == 4 and pending[0] is not None:
                flush_dance()
            if kc == KCH - 1:
                P.wait("vector", sPE, pcons[c])
                P.wait("vector", sCa, prev_repl_pe[0])
                P.op("vector", lambda e, Lb=Lb: e.tensor_copy(l_sp[0:1, :], Lb[0:1, :]))
                P.op("vector", lambda e, Lb=Lb: e.tensor_copy(l_sp[32:33, :], Lb[32:33, :]), inc=sVE)
                pending[0] = (Lb, PVb, pi, p, qb, P.val(sVE))
            if j == NCH - 1:
                pair_done_pe[p] = P.val(sPE)
                if p == PAIRS - 1 and pending[0] is not None:
                    flush_dance_pe()
                if p + 2 < PAIRS:
                    emit_kv_load(p + 2, pair_done_pe[p])
        b_ve = P.val(sVE)

        if stage == "B":
            dummy_out()
            finish_emit()
            return nc

        wot_v = ub(0, 8192)
        t1o = ub(4096, 2048)
        bo_sb = uf(19456, 1024)
        xf2 = [uf(5120, 1024), uf(6144, 1024)]
        xbo = uf(7168, 1024)
        oscr = uf(8192, 512)
        fscr = uf(8704, 512)
        outst = [uf(9216, 512), uf(9728, 512)]

        wot_r = wot_d.rearrange("(c p) o -> p c o", p=128)
        P.wait("gpsimd", sPE, pair_done_pe[6])
        P.op("gpsimd", lambda e: e.dma_start(
            out=wot_v[:, 0:4096].rearrange("p (c o) -> p c o", c=4),
            in_=wot_r[:, 0:4, :]), inc=sCa, inck=16)
        P.wait("gpsimd", sPE, pair_done_pe[7])
        P.op("gpsimd", lambda e: e.dma_start(
            out=wot_v[:, 4096:8192].rearrange("p (c o) -> p c o", c=4),
            in_=wot_r[:, 4:8, :]), inc=sCa, inck=16)
        wot_ca = P.val(sCa)
        P.wait("sync", sVE, b_ve)
        P.op("sync", lambda e: e.dma_start(out=bo_sb[:, :], in_=bof_d), inc=sLd, inck=16)
        bo_ld = P.val(sLd)

        if stage == "Cw":
            dummy_out()
            finish_emit()
            return nc

        P.wait("tensor", sVE, b_ve)
        t1o_evac = {}
        for blk in range(4):
            r0 = 32 * (blk % 2)
            if blk >= 2:
                P.wait("tensor", sVE, t1o_evac[blk - 2])
            for oc in range(8):
                P.op("tensor", lambda e, blk=blk, oc=oc, r0=r0: e.matmul(
                    ps_pv[r0:r0 + 16, :], aot_s[:, oc * 16:(oc + 1) * 16],
                    numt[:, oc * 2048 + blk * 512: oc * 2048 + (blk + 1) * 512],
                    start=(oc == 0), stop=(oc == 7),
                    tile_position=(0, r0)), inc=(sPE if oc == 7 else None))
            P.wait("vector", sPE, P.val(sPE))
            P.op("vector", lambda e, blk=blk, r0=r0: e.tensor_scalar_mul(
                t1o[0:16, blk * 512:(blk + 1) * 512], ps_pv[r0:r0 + 16, :],
                1.0 / 16.0), inc=sVE)
            t1o_evac[blk] = P.val(sVE)
        t1o_ve = P.val(sVE)

        xf_ld = {}
        xbo_ve = {}

        def emit_xf(tt):
            if tt >= 2:
                P.wait("sync", sVE, xbo_ve[tt - 2])
            else:
                P.wait("sync", sPE, pair_done_pe[7])
            P.op("sync", lambda e, tt=tt: e.dma_start(
                out=xf2[tt % 2][:, :], in_=x_r[:, tt, :]), inc=sLd, inck=16)
            xf_ld[tt] = P.val(sLd)

        emit_xf(0)
        emit_xf(1)

        if stage == "C0":
            dummy_out()
            finish_emit()
            return nc

        P.wait("tensor", sCa, wot_ca)
        o_evac = {}
        st_done = {}
        ntiles = 1 if stage == "C1" else 16
        for tt in range(ntiles):
            P.wait("vector", sLd, max(xf_ld[tt], bo_ld))
            P.op("vector", lambda e, tt=tt: e.tensor_add(
                xbo[:, :], xf2[tt % 2][:, :], bo_sb[:, :]), inc=sVE)
            xbo_ve[tt] = P.val(sVE)
            for dh in range(2):
                fps = ps_s1[:, dh * 512:(dh + 1) * 512]
                ops_ = ps_s0[:, dh * 512:(dh + 1) * 512]
                P.wait("tensor", sVE, o_evac.get((tt - 1, dh), 0))
                for ic in range(8):
                    P.op("tensor", lambda e, ic=ic, tt=tt, dh=dh, fps=fps: e.matmul(
                        fps, qlt[:, ic * 2048 + tt * 128: ic * 2048 + (tt + 1) * 128],
                        qrh[:, ic * 1024 + dh * 512: ic * 1024 + (dh + 1) * 512],
                        start=(ic == 0), stop=(ic == 7)))
                for oc in range(8):
                    P.op("tensor", lambda e, oc=oc, tt=tt, dh=dh, ops_=ops_: e.matmul(
                        ops_, numt[:, oc * 2048 + tt * 128: oc * 2048 + (tt + 1) * 128],
                        wot_v[:, oc * 1024 + dh * 512: oc * 1024 + (dh + 1) * 512],
                        start=(oc == 0), stop=False))
                if tt == 0 and dh == 0:
                    P.wait("tensor", sVE, t1o_ve)
                P.op("tensor", lambda e, tt=tt, dh=dh, ops_=ops_: e.matmul(
                    ops_, t1o[0:16, tt * 128:(tt + 1) * 128],
                    bot_s[0:16, dh * 512:(dh + 1) * 512],
                    start=False, stop=True), inc=sPE)
                mm_pe = P.val(sPE)
                P.wait("vector", sPE, mm_pe)
                P.wait("vector", sVE, xbo_ve[tt])
                P.op("vector", lambda e, tt=tt, dh=dh, fps=fps: e.scalar_tensor_tensor(
                    fscr[:, :], fps, ls_all[:, tt:tt + 1],
                    rs_full[:, dh * 512:(dh + 1) * 512], AL.mult, AL.mult), inc=sVE)
                P.op("vector", lambda e, dh=dh, ops_=ops_: e.scalar_tensor_tensor(
                    oscr[:, :], ops_, 1.0, xbo[:, dh * 512:(dh + 1) * 512],
                    AL.bypass, AL.add), inc=sVE)
                o_evac[(tt, dh)] = P.val(sVE)
                P.wait("vector", sVE, P.val(sVE))
                ob = outst[dh]
                P.wait("vector", sSt, st_done.get((tt - 1, dh), 0))
                P.op("vector", lambda e, ob=ob: e.tensor_add(ob[:, :], fscr[:, :], oscr[:, :]), inc=sVE)
                P.wait("sync", sVE, P.val(sVE))
                P.op("sync", lambda e, tt=tt, dh=dh, ob=ob: e.dma_start(
                    out=out_r[:, tt, dh * 512:(dh + 1) * 512], in_=ob[:, :]), inc=sSt, inck=16)
                st_done[(tt, dh)] = P.val(sSt)
            if tt + 2 < 16:
                emit_xf(tt + 2)

        if stage == "C1":
            dummy_out()
            finish_emit()
            return nc

        P.wait("sync", sSt, P.val(sSt))

        finish_emit()

    return nc


def _stack_a(aq, ak, av):
    a96 = np.zeros((96, D), dtype=np.float32)
    a96[0:16] = aq
    a96[32:48] = ak
    a96[64:80] = av
    return a96.T


def kernel(**inputs):
    x = np.asarray(inputs["L0_attn_x"])
    wq = np.asarray(inputs["L0_attn_Wq"])
    wk = np.asarray(inputs["L0_attn_Wk"])
    wv = np.asarray(inputs["L0_attn_Wv"])
    wo = np.asarray(inputs["L0_attn_Wo"])
    bo = np.asarray(inputs["L0_attn_bo"])
    aq = np.asarray(inputs["L0_attn_Aq"])
    bq = np.asarray(inputs["L0_attn_Bq"])
    ak = np.asarray(inputs["L0_attn_Ak"])
    bk = np.asarray(inputs["L0_attn_Bk"])
    av = np.asarray(inputs["L0_attn_Av"])
    bv = np.asarray(inputs["L0_attn_Bv"])
    ao = np.asarray(inputs["L0_attn_Ao"])
    bo_l = np.asarray(inputs["L0_attn_Bo"])
    rhs = np.asarray(inputs["L0_ffn_rhs"])

    C = np.ascontiguousarray
    shared = {
        "wqt": C(wq.T), "wkt": C(wk.T), "wvt": C(wv.T), "wot": C(wo.T),
        "aqkvt": C(_stack_a(aq, ak, av)), "aot": C(ao.T),
        "bqt": C(bq.T), "bkt": C(bk.T), "bvt": C(bv.T), "bot": C(bo_l.T),
        "rhst": C(rhs.T),
        "bo_full": C(np.broadcast_to(bo, (128, D)).copy()),
    }

    in_maps = []
    for c in range(8):
        b, half = c // 2, c % 2
        own = x[b, half * 2048:(half + 1) * 2048]
        oth = x[b, (1 - half) * 2048:(2 - half) * 2048]
        m = dict(shared)
        xcat = np.concatenate([own, oth], axis=0)
        m["x"] = C(xcat.reshape(32, 128, D).transpose(1, 0, 2).reshape(NSEQ, D))
        in_maps.append(m)

    nc = build_nc()
    res = run_bass_kernel_spmd(nc, in_maps, core_ids=list(range(8)))

    out = np.empty((4, NSEQ, D), dtype=np.float32)
    for c in range(8):
        b, half = c // 2, c % 2
        out[b, half * 2048:(half + 1) * 2048] = res.results[c]["out"]
    return out
